# version 09
# baseline (speedup 1.0000x reference)
import contextlib
import numpy as np
import ml_dtypes
import concourse.bass as bass
import concourse.mybir as mybir
from concourse.bass_utils import run_bass_kernel_spmd

F32 = mybir.dt.float32
BF16 = mybir.dt.bfloat16
I32 = mybir.dt.int32
AF = mybir.ActivationFunctionType
ALU = mybir.AluOpType
AX = mybir.AxisListType

T = 2048
DM = 4096
NK = DM // 128
DQ = 4096
DKV = 1024
H = 32
G = 8
D = 128
DFF = 11008
NFF = DFF // 128
NT = T // 128
NSL = T // 512
EPS = 1e-5
NEG = -1e30


class Sem:
    __slots__ = ("h", "v")

    def __init__(self, h):
        self.h = h
        self.v = 0

    def inc(self, instr, n=1):
        if "DMA" in type(instr.ins).__name__ and n < 16:
            n = 16
        instr.then_inc(self.h, n)
        self.v += n
        return self.v

    def plan(self, n=1):
        self.v += n
        return self.v

    def inc_planned(self, instr, n=1):
        instr.then_inc(self.h, n)


def build_program(nphases=6, dbg=()):
    nc = bass.Bass("TRN2", target_bir_lowering=False)
    es = contextlib.ExitStack()

    def dram(name, shape, dtype, kind):
        if kind == "Internal" and name in dbg:
            kind = "ExternalOutput"
        return nc.dram_tensor(name, shape, dtype, kind=kind).ap()

    x_d = dram("x", [T, DM], F32, "ExternalInput")
    wq_d = dram("wq", [DM, DQ], F32, "ExternalInput")
    wk_d = dram("wk", [DM, DKV], F32, "ExternalInput")
    wv_d = dram("wv", [DM, DKV], F32, "ExternalInput")
    wo_d = dram("wo", [DQ, DM], F32, "ExternalInput")
    bq_d = dram("bq2", [128, H], F32, "ExternalInput")
    bk_d = dram("bk2", [128, G], F32, "ExternalInput")
    bv_d = dram("bv2", [128, G], F32, "ExternalInput")
    bo_d = dram("bo_b", [128, DM], F32, "ExternalInput")
    gw_d = dram("gw", [DFF, DM], I32, "ExternalInput")
    uw_d = dram("uw", [DFF, DM], I32, "ExternalInput")
    dw_d = dram("dw", [DM, DFF], I32, "ExternalInput")
    gs_d = dram("gs2", [128, NFF], F32, "ExternalInput")
    us_d = dram("us2", [128, NFF], F32, "ExternalInput")
    ds_d = dram("ds_b", [128, DM], F32, "ExternalInput")
    id_d = dram("ident", [128, 128], BF16, "ExternalInput")
    qcos_d = dram("qcos", [128, T], F32, "ExternalInput")
    qsin_d = dram("qsin", [128, T], F32, "ExternalInput")
    kcos_d = dram("kcos", [128, T], F32, "ExternalInput")
    ksin_d = dram("ksin", [128, T], F32, "ExternalInput")
    mask_d = dram("masks", [128, 4, 512], BF16, "ExternalInput")

    out_d = dram("out", [T, DM], F32, "ExternalOutput")

    q_sp = dram("q_sp", [H * 128, T], BF16, "Internal")
    k_sp = dram("k_sp", [G * 128, T], BF16, "Internal")
    v_sp = dram("v_sp", [G * 128, T], BF16, "Internal")
    h_sp = dram("h_sp", [NFF * 128, T], BF16, "Internal")
    dwT_sp = dram("dwT_sp", [8 * NFF * 128, 512], BF16, "Internal")
    wob_sp = dram("wob_sp", [DQ, DM], BF16, "Internal")
    s_sp = dram("s_sp", [T, DM], BF16, "Internal")

    sQ = Sem(es.enter_context(nc.semaphore("sQ")))
    sSc = Sem(es.enter_context(nc.semaphore("sSc")))
    sG = Sem(es.enter_context(nc.semaphore("sG")))
    sPE = Sem(es.enter_context(nc.semaphore("sPE")))
    sV = Sem(es.enter_context(nc.semaphore("sV")))
    sA = Sem(es.enter_context(nc.semaphore("sA")))
    sBar = Sem(es.enter_context(nc.semaphore("sBar")))

    ident_sb = es.enter_context(nc.sbuf_tensor("ident_sb", [128, 128], BF16))
    xn_es = contextlib.ExitStack()
    xnT = xn_es.enter_context(nc.sbuf_tensor("xnT", [128, NK, T], BF16))

    block_cm = nc.Block()
    block = block_cm.__enter__()

    ENG = ["tensor", "vector", "scalar", "gpsimd", "sync"]

    def barrier():
        for e in ENG:
            def mk(e=e):
                def f(eng):
                    sBar.inc(eng.nop())
                return f
            getattr(block, e)(mk())
        tgt = sBar.v
        for e in ENG:
            def mk2(e=e):
                def f(eng):
                    eng.wait_ge(sBar.h, tgt)
                return f
            getattr(block, e)(mk2())

    def f_const(sy):
        sQ.inc(sy.dma_start(out=ident_sb[:], in_=id_d))
    block.sync(f_const)

    if nphases >= 1:
        p0 = contextlib.ExitStack()
        x_b = [p0.enter_context(nc.sbuf_tensor(f"p0x{i}", [128, DM], F32)) for i in range(2)]
        xn_nat = [p0.enter_context(nc.sbuf_tensor(f"p0n{i}", [128, DM], BF16)) for i in range(2)]
        xsq = [p0.enter_context(nc.sbuf_tensor(f"p0q{i}", [128, DM], BF16)) for i in range(2)]
        rs = [p0.enter_context(nc.sbuf_tensor(f"p0rs{i}", [128, 1], F32)) for i in range(2)]
        nm = [p0.enter_context(nc.sbuf_tensor(f"p0nm{i}", [128, 1], F32)) for i in range(2)]
        sq = [p0.enter_context(nc.sbuf_tensor(f"p0sq{i}", [128, 1], F32)) for i in range(2)]
        lnv = [p0.enter_context(nc.sbuf_tensor(f"p0ln{i}", [128, 1], F32)) for i in range(2)]
        rstd = [p0.enter_context(nc.sbuf_tensor(f"p0rd{i}", [128, 1], F32)) for i in range(2)]
        eps_sb = p0.enter_context(nc.sbuf_tensor("p0eps", [128, 1], F32))
        tps = p0.enter_context(nc.psum_tensor("p0t", [128, 2, 4, 128], F32))

        def f_eps(v):
            sV.inc(v.memset(eps_sb[:], EPS))
        block.vector(f_eps)
        eps_ready = sV.v

        xload = [0, 0]
        xn_done = [0, 0]
        sq_done = [0, 0]
        pe_tile = [0, 0]
        ev_grp = [0, 0]
        sums_done = [0, 0]

        for i in range(NT):
            p = i % 2

            def f_sync(i=i, p=p):
                def f(sy):
                    if i >= 2:
                        sy.wait_ge(sV.h, xn_done[p])
                    sQ.inc(sy.dma_start(out=x_b[p][:], in_=x_d[i * 128:(i + 1) * 128, :]))
                    xload[p] = sQ.v
                return f
            block.sync(f_sync())

            def f_v1(i=i, p=p):
                def f(v):
                    v.wait_ge(sQ.h, xload[p])
                    if i >= 2:
                        v.wait_ge(sPE.h, pe_tile[p])
                    sV.inc(v.tensor_reduce(out=rs[p][:], in_=x_b[p][:], axis=AX.X, op=ALU.add))
                    v.wait_ge(sV.h, sV.v)
                    sV.inc(v.tensor_scalar(out=nm[p][:], in0=rs[p][:], scalar1=1.0 / DM,
                                           scalar2=None, op0=ALU.mult))
                    v.wait_ge(sV.h, sV.v)
                    sV.inc(v.tensor_scalar(out=x_b[p][:], in0=x_b[p][:], scalar1=nm[p][:],
                                           scalar2=None, op0=ALU.subtract))
                return f
            block.vector(f_v1())
            xc_done = sV.v

            def f_gq(i=i, p=p, xc_done=xc_done):
                def f(g):
                    g.wait_ge(sV.h, xc_done)
                    sG.inc(g.tensor_tensor(out=xsq[p][:], in0=x_b[p][:], in1=x_b[p][:],
                                           op=ALU.mult), 1)
                return f
            block.gpsimd(f_gq())
            gq_done = sG.v

            def f_v1b(i=i, p=p, gq_done=gq_done):
                def f(v):
                    v.wait_ge(sG.h, gq_done)
                    sV.inc(v.tensor_reduce(out=sq[p][:], in_=xsq[p][:], axis=AX.X, op=ALU.add))
                    sums_done[p] = sV.v
                return f
            block.vector(f_v1b())
            sumsq_done = sV.v

            def f_a1(i=i, p=p, sumsq_done=sumsq_done):
                def f(s):
                    s.wait_ge(sV.h, sumsq_done)
                    if i < 2:
                        s.wait_ge(sV.h, eps_ready)
                    sA.inc(s.activation(out=lnv[p][:], in_=sq[p][:], func=AF.Ln,
                                        scale=1.0 / DM, bias=eps_sb[:]))
                    s.wait_ge(sA.h, sA.v)
                    sA.inc(s.activation(out=rstd[p][:], in_=lnv[p][:], func=AF.Exp, scale=-0.5))
                return f
            block.scalar(f_a1())
            rstd_done = sA.v

            def f_v2(i=i, p=p, rstd_done=rstd_done):
                def f(v):
                    v.wait_ge(sA.h, rstd_done)
                    sV.inc(v.tensor_scalar(out=xn_nat[p][:], in0=x_b[p][:], scalar1=rstd[p][:],
                                           scalar2=None, op0=ALU.mult))
                    xn_done[p] = sV.v
                return f
            block.vector(f_v2())

            for grp in range(8):
                tb = grp % 2

                def f_pe(i=i, p=p, grp=grp, tb=tb):
                    def f(t):
                        if grp == 0:
                            t.wait_ge(sV.h, xn_done[p])
                            t.wait_ge(sQ.h, xload[p])
                        t.wait_ge(sV.h, ev_grp[tb])
                        for j in range(4):
                            c = grp * 4 + j
                            ins = t.matmul(tps[:, tb, j, :], xn_nat[p][:, c * 128:(c + 1) * 128],
                                           ident_sb[:], start=True, stop=True)
                        sPE.inc(ins)
                    return f
                block.tensor(f_pe())
                pe_cnt = sPE.v

                def f_ev(i=i, grp=grp, tb=tb, pe_cnt=pe_cnt):
                    def f(v):
                        v.wait_ge(sPE.h, pe_cnt)
                        ev_grp[tb] = sV.inc(v.tensor_copy(
                            out=xnT[:, grp * 4:(grp + 1) * 4, i * 128:(i + 1) * 128],
                            in_=tps[:, tb, :, :]))
                    return f
                block.vector(f_ev())
            pe_tile[p] = sPE.v
        barrier()
        p0.close()

    if nphases >= 2:
        p1 = contextlib.ExitStack()
        wsl = [p1.enter_context(nc.sbuf_tensor(f"p1w{i}", [128, NK, 128], BF16)) for i in range(2)]
        osb = [p1.enter_context(nc.sbuf_tensor(f"p1o{i}", [128, T], BF16)) for i in range(2)]
        swp = p1.enter_context(nc.sbuf_tensor("p1swp", [128, T], BF16))
        t1 = p1.enter_context(nc.sbuf_tensor("p1t1", [128, T], BF16))
        t2 = p1.enter_context(nc.sbuf_tensor("p1t2", [128, T], BF16))
        cos_sb = p1.enter_context(nc.sbuf_tensor("p1cos", [128, T], F32))
        sin_sb = p1.enter_context(nc.sbuf_tensor("p1sin", [128, T], F32))
        bq_sb = p1.enter_context(nc.sbuf_tensor("p1bq", [128, H], F32))
        bk_sb = p1.enter_context(nc.sbuf_tensor("p1bk", [128, G], F32))
        bv_sb = p1.enter_context(nc.sbuf_tensor("p1bv", [128, G], F32))
        qps = [p1.enter_context(nc.psum_tensor(f"p1p{i}", [128, 4, 512], F32)) for i in range(2)]

        specs = []
        for c in range(H):
            specs.append((wq_d, c, bq_sb, c, "q", q_sp, c))
        for g in range(G):
            specs.append((wk_d, g, bk_sb, g, "k", k_sp, g))
        for g in range(G):
            specs.append((wv_d, g, bv_sb, g, None, v_sp, g))

        def f_ld0():
            def f(sy):
                sQ.inc(sy.dma_start(out=bq_sb[:], in_=bq_d))
                sQ.inc(sy.dma_start(out=bk_sb[:], in_=bk_d))
                sQ.inc(sy.dma_start(out=bv_sb[:], in_=bv_d))
                sQ.inc(sy.dma_start(out=cos_sb[:], in_=qcos_d))
                sQ.inc(sy.dma_start(out=sin_sb[:], in_=qsin_d))
            return f
        block.sync(f_ld0())
        tab_ready = sQ.v
        bias_ready = sQ.v

        wload = [0, 0]
        mm_done = [0, 0]
        evac_done = [0, 0]
        spill_done = [0, 0]
        rope_last = [0]
        t2_done = [0]
        last_q_rope = [0]

        for j, (w_d, col, b_sb, bidx, rope, sp_d, spc) in enumerate(specs):
            p = j % 2

            def f_g(j=j, p=p, w_d=w_d, col=col):
                def f(g):
                    if j >= 2:
                        g.wait_ge(sPE.h, mm_done[p])
                    src = w_d[:, col * 128:(col + 1) * 128].rearrange("(k p) c -> p k c", p=128)
                    sG.inc(g.dma_start(out=wsl[p][:], in_=src), 16)
                    wload[p] = sG.v
                return f
            block.gpsimd(f_g())

            def f_pe(j=j, p=p):
                def f(t):
                    t.wait_ge(sG.h, wload[p])
                    if j >= 2:
                        t.wait_ge(sA.h, evac_done[p])
                    for k in range(NK):
                        for ts in range(4):
                            ins = t.matmul(qps[p][:, ts, :], wsl[p][:, k, :],
                                           xnT[:, k, ts * 512:(ts + 1) * 512],
                                           start=(k == 0), stop=(k == NK - 1))
                    sPE.inc(ins)
                    mm_done[p] = sPE.v
                return f
            block.tensor(f_pe())
            mm_cnt = sPE.v

            def f_a(j=j, p=p, b_sb=b_sb, bidx=bidx, mm_cnt=mm_cnt):
                def f(s):
                    s.wait_ge(sPE.h, mm_cnt)
                    if j == 0:
                        s.wait_ge(sQ.h, bias_ready)
                    if j >= 2:
                        s.wait_ge(sSc.h, spill_done[p])
                    for ts in range(4):
                        sA.inc(s.activation(out=osb[p][:, ts * 512:(ts + 1) * 512],
                                            in_=qps[p][:, ts, :], func=AF.Identity,
                                            bias=b_sb[:, bidx:bidx + 1]))
                    evac_done[p] = sA.v
                return f
            block.scalar(f_a())
            ev_cnt = sA.v

            if rope is not None:
                def f_sw(j=j, p=p, ev_cnt=ev_cnt):
                    def f(s):
                        s.wait_ge(sA.h, ev_cnt)
                        s.wait_ge(sV.h, t2_done[0])
                        sSc.inc(s.dma_start(out=swp[0:64, :], in_=osb[p][64:128, :]), 16)
                        sSc.inc(s.dma_start(out=swp[64:128, :], in_=osb[p][0:64, :]), 16)
                    return f
                block.scalar(f_sw())
                sw_cnt = sSc.v

                def f_rope(j=j, p=p, ev_cnt=ev_cnt, sw_cnt=sw_cnt, rope=rope):
                    def f(v):
                        v.wait_ge(sA.h, ev_cnt)
                        v.wait_ge(sSc.h, sw_cnt)
                        if j < 2 or j in (H, H + 1):
                            v.wait_ge(sQ.h, tab_ready)
                        v.wait_ge(sV.h, rope_last[0])
                        sV.inc(v.tensor_tensor(out=t1[:], in0=osb[p][:], in1=cos_sb[:], op=ALU.mult))
                        sV.inc(v.tensor_tensor(out=t2[:], in0=swp[:], in1=sin_sb[:], op=ALU.mult))
                        t2_done[0] = sV.v
                        v.wait_ge(sV.h, sV.v)
                        sV.inc(v.tensor_tensor(out=osb[p][:], in0=t1[:], in1=t2[:], op=ALU.add))
                        rope_last[0] = sV.v
                        if rope == "q":
                            last_q_rope[0] = sV.v
                    return f
                block.vector(f_rope())
                rdy = sV.v
            else:
                rdy = None

            def f_sp(j=j, p=p, sp_d=sp_d, spc=spc, rdy=rdy, ev_cnt=ev_cnt):
                def f(s):
                    if rdy is not None:
                        s.wait_ge(sV.h, rdy)
                    else:
                        s.wait_ge(sA.h, ev_cnt)
                    sSc.inc(s.dma_start(out=sp_d[spc * 128:(spc + 1) * 128, :], in_=osb[p][:]), 16)
                    spill_done[p] = sSc.v
                return f
            block.scalar(f_sp())

            if j == H - 1:
                def f_tb():
                    def f(sy):
                        sy.wait_ge(sV.h, last_q_rope[0])
                        sQ.inc(sy.dma_start(out=cos_sb[:], in_=kcos_d))
                        sQ.inc(sy.dma_start(out=sin_sb[:], in_=ksin_d))
                    return f
                block.sync(f_tb())
                tab_ready = sQ.v
        barrier()
        p1.close()

    if nphases >= 3:
        p2 = contextlib.ExitStack()
        nat = [p2.enter_context(nc.sbuf_tensor(f"p2n{i}", [128, DM], BF16)) for i in range(2)]
        gT = p2.enter_context(nc.sbuf_tensor("p2gT", [128, NK, 128], BF16))
        uT = p2.enter_context(nc.sbuf_tensor("p2uT", [128, NK, 128], BF16))
        t1g = p2.enter_context(nc.sbuf_tensor("p2t1", [128, T], BF16))
        t2u = [p2.enter_context(nc.sbuf_tensor(f"p2t2{i}", [128, 512], BF16)) for i in range(2)]
        hob = [p2.enter_context(nc.sbuf_tensor(f"p2h{i}", [128, 512], BF16)) for i in range(2)]
        gs_sb = p2.enter_context(nc.sbuf_tensor("p2gs", [128, NFF], F32))
        us_sb = p2.enter_context(nc.sbuf_tensor("p2us", [128, NFF], F32))
        tps2 = p2.enter_context(nc.psum_tensor("p2t", [128, 2, 4, 128], F32))
        gps = p2.enter_context(nc.psum_tensor("p2g", [128, 2, 512], F32))
        ups = p2.enter_context(nc.psum_tensor("p2u", [128, 2, 512], F32))

        def f_ld():
            def f(sy):
                sQ.inc(sy.dma_start(out=gs_sb[:], in_=gs_d))
                sQ.inc(sy.dma_start(out=us_sb[:], in_=us_d))
            return f
        block.sync(f_ld())
        gs_ready = sQ.v

        natload = [0, 0]
        natfree_pe = [0, 0]
        ev_tb = [0, 0]
        mmu_done = [0]
        mmg_done = [0]
        silu_done = [0, 0, 0, 0]
        t2_ts = [0, 0]
        h_ts = [0, 0]
        hsp_ts = [0, 0]
        gbank_free = [0, 0]
        ubank_free = [0, 0]

        def emit_T(src_par, dst_sb, prev_reader_cnt):
            evs = []
            for grp in range(8):
                tb = grp % 2

                def f_pe(grp=grp, tb=tb, src_par=src_par):
                    def fn(t):
                        if grp == 0:
                            t.wait_ge(sG.h, natload[src_par])
                        t.wait_ge(sV.h, ev_tb[tb])
                        for jj in range(4):
                            cc = grp * 4 + jj
                            ins = t.matmul(tps2[:, tb, jj, :], nat[src_par][:, cc * 128:(cc + 1) * 128],
                                           ident_sb[:], start=True, stop=True)
                        sPE.inc(ins)
                    return fn
                block.tensor(f_pe())
                pec = sPE.v

                def f_ev(grp=grp, tb=tb, dst_sb=dst_sb, pec=pec, prev=prev_reader_cnt):
                    def fn(v):
                        v.wait_ge(sPE.h, pec)
                        v.wait_ge(sPE.h, prev)
                        ev_tb[tb] = sV.inc(v.tensor_copy(
                            out=dst_sb[:, grp * 4:(grp + 1) * 4, :], in_=tps2[:, tb, :, :]))
                        evs.append(ev_tb[tb])
                    return fn
                block.vector(f_ev())
            return evs

        for f in range(NFF):
            for which, w_dd in ((0, gw_d), (1, uw_d)):
                def f_g(f=f, which=which, w_dd=w_dd):
                    def fn(g):
                        if f >= 1:
                            g.wait_ge(sPE.h, natfree_pe[which])
                        sG.inc(g.dma_start(out=nat[which][:],
                                           in_=w_dd[f * 128:(f + 1) * 128, :],
                                           max_dma_last_dim=8192), 16)
                        natload[which] = sG.v
                    return fn
                block.gpsimd(f_g())

            evs_g = emit_T(0, gT, mmg_done[0])
            natfree_pe[0] = sPE.v
            evs_u = emit_T(1, uT, mmu_done[0])
            natfree_pe[1] = sPE.v

            silu_defer = []
            for ts in range(4):
                pb = ts % 2

                def f_mmg(f=f, ts=ts, pb=pb, evg=max(evs_g)):
                    def fn(t):
                        if ts == 0:
                            t.wait_ge(sV.h, evg)
                        t.wait_ge(sA.h, gbank_free[pb])
                        for k in range(NK):
                            ins = t.matmul(gps[:, pb, :], gT[:, k, :],
                                           xnT[:, k, ts * 512:(ts + 1) * 512],
                                           start=(k == 0), stop=(k == NK - 1))
                        sPE.inc(ins)
                    return fn
                block.tensor(f_mmg())
                mmc = sPE.v

                planned = sA.plan(1)

                def d_silu(f=f, ts=ts, pb=pb, mmc=mmc, hfree=h_ts[ts % 2],
                           first=(f == 0 and ts == 0)):
                    def fn(s):
                        s.wait_ge(sPE.h, mmc)
                        if first:
                            s.wait_ge(sQ.h, gs_ready)
                        s.wait_ge(sV.h, hfree)
                        ins = s.activation(out=t1g[:, ts * 512:(ts + 1) * 512], in_=gps[:, pb, :],
                                           func=AF.Silu, scale=gs_sb[:, f:f + 1])
                        sA.inc_planned(ins)
                    return fn
                silu_defer.append(d_silu())
                gbank_free[pb] = planned
                silu_done[ts] = planned
            mmg_done[0] = sPE.v

            def f_silu_flush(fns=tuple(silu_defer)):
                def fn(s):
                    for g_ in fns:
                        g_(s)
                return fn
            block.scalar(f_silu_flush())

            for ts in range(4):
                pb = ts % 2

                def f_mmu(f=f, ts=ts, pb=pb, evu=max(evs_u)):
                    def fn(t):
                        if ts == 0:
                            t.wait_ge(sV.h, evu)
                        t.wait_ge(sV.h, ubank_free[pb])
                        for k in range(NK):
                            ins = t.matmul(ups[:, pb, :], uT[:, k, :],
                                           xnT[:, k, ts * 512:(ts + 1) * 512],
                                           start=(k == 0), stop=(k == NK - 1))
                        sPE.inc(ins)
                    return fn
                block.tensor(f_mmu())
                mmc = sPE.v

                def f_ep(f=f, ts=ts, pb=pb, mmc=mmc, sdone=silu_done[ts]):
                    def fn(v):
                        v.wait_ge(sPE.h, mmc)
                        if f == 0 and ts == 0:
                            v.wait_ge(sQ.h, gs_ready)
                        v.wait_ge(sSc.h, hsp_ts[ts % 2])
                        ins = v.tensor_scalar(out=t2u[ts % 2][:], in0=ups[:, pb, :],
                                              scalar1=us_sb[:, f:f + 1], scalar2=None, op0=ALU.mult)
                        ubank_free[pb] = sV.inc(ins)
                        t2_ts[ts % 2] = sV.v
                        v.wait_ge(sV.h, sV.v)
                        v.wait_ge(sA.h, sdone)
                        ins = v.tensor_tensor(out=hob[ts % 2][:], in0=t1g[:, ts * 512:(ts + 1) * 512],
                                              in1=t2u[ts % 2][:], op=ALU.mult)
                        h_ts[ts % 2] = sV.inc(ins)
                    return fn
                block.vector(f_ep())
                hcnt = sV.v

                def f_hsp(f=f, ts=ts, hcnt=hcnt):
                    def fn(s):
                        s.wait_ge(sV.h, hcnt)
                        sSc.inc(s.dma_start(out=h_sp[f * 128:(f + 1) * 128, ts * 512:(ts + 1) * 512],
                                            in_=hob[ts % 2][:]), 16)
                        hsp_ts[ts % 2] = sSc.v
                    return fn
                block.scalar(f_hsp())
            mmu_done[0] = sPE.v
        barrier()
        p2.close()
        xn_es.close()

    if nphases >= 4:
        p3 = contextlib.ExitStack()
        csl = [[p3.enter_context(nc.sbuf_tensor(f"p3c{s}_{j}", [128, DFF], BF16)) for j in range(4)]
               for s in range(2)]
        stg = [p3.enter_context(nc.sbuf_tensor(f"p3s{i}", [128, 8, 512], BF16)) for i in range(4)]
        tps3 = p3.enter_context(nc.psum_tensor("p3t", [128, 2, 4, 128], F32))

        def f_wo():
            def f(g):
                for i in range(8):
                    sG.inc(g.dma_start(out=wob_sp[i * 512:(i + 1) * 512, :],
                                       in_=wo_d[i * 512:(i + 1) * 512, :],
                                       max_dma_last_dim=8192), 16)
            return f
        block.gpsimd(f_wo())
        wo_pre = sG.v

        dsload = [0, 0]
        dsfree = [0, 0]
        ev3 = [0, 0]
        stg_wr = [0, 0, 0, 0]
        stg_ev = [0, 0, 0, 0]

        fgroups = [(a, min(8, NFF - a)) for a in range(0, NFF, 8)]

        for ds in range(8):
            sp = ds % 2

            def f_ld(ds=ds, sp=sp):
                def fn(g):
                    if ds >= 2:
                        g.wait_ge(sPE.h, dsfree[sp])
                    for j in range(4):
                        sG.inc(g.dma_start(out=csl[sp][j][:],
                                           in_=dw_d[(ds * 4 + j) * 128:(ds * 4 + j + 1) * 128, :],
                                           max_dma_last_dim=8192), 16)
                    dsload[sp] = sG.v
                return fn
            block.gpsimd(f_ld())

            for gi, (f0, nf) in enumerate(fgroups):
                bs = gi % 4

                for fi in range(nf):
                    f = f0 + fi
                    tb = f % 2

                    def f_pe(ds=ds, sp=sp, f=f, tb=tb, gi=gi, fi=fi, bs=bs):
                        def fn(t):
                            if f == 0:
                                t.wait_ge(sG.h, dsload[sp])
                            t.wait_ge(sV.h, ev3[tb])
                            for j in range(4):
                                ins = t.matmul(tps3[:, tb, j, :], csl[sp][j][:, f * 128:(f + 1) * 128],
                                               ident_sb[:], start=True, stop=True)
                            sPE.inc(ins)
                        return fn
                    block.tensor(f_pe())
                    pec = sPE.v

                    def f_ev(tb=tb, fi=fi, bs=bs, pec=pec):
                        def fn(v):
                            v.wait_ge(sPE.h, pec)
                            v.wait_ge(sSc.h, stg_wr[bs])
                            ev3[tb] = sV.inc(v.tensor_copy(out=stg[bs][:, fi, :],
                                                           in_=tps3[:, tb, :, :]))
                            stg_ev[bs] = ev3[tb]
                        return fn
                    block.vector(f_ev())

                def f_wr(ds=ds, f0=f0, nf=nf, bs=bs, ev=stg_ev, gi=gi):
                    cnt = stg_ev[bs]

                    def fn(s):
                        s.wait_ge(sV.h, cnt)
                        r0 = (ds * NFF + f0) * 128
                        dst = dwT_sp[r0:r0 + nf * 128, :].rearrange("(f p) d -> p f d", p=128)
                        sSc.inc(s.dma_start(out=dst, in_=stg[bs][:, 0:nf, :]), 16)
                        stg_wr[bs] = sSc.v
                    return fn
                block.scalar(f_wr())
            dsfree[sp] = sPE.v
        barrier()
        p3.close()
        p3_sc_end = sSc.v

    if nphases >= 5:
        p4 = contextlib.ExitStack()
        k_sb = p4.enter_context(nc.sbuf_tensor("p4k", [128, G, T], BF16))
        v_sb = p4.enter_context(nc.sbuf_tensor("p4v", [128, NT, G, 128], BF16))
        vt_b = p4.enter_context(nc.sbuf_tensor("p4vt", [128, T], BF16))
        qb = [p4.enter_context(nc.sbuf_tensor(f"p4q{i}", [128, 512], BF16)) for i in range(2)]
        pn = [p4.enter_context(nc.sbuf_tensor(f"p4pn{i}", [128, T], BF16)) for i in range(2)]
        pTs = p4.enter_context(nc.sbuf_tensor("p4pT", [128, 16, 512], BF16))
        ctxT = p4.enter_context(nc.sbuf_tensor("p4ctx", [128, 2, H, 512], BF16))
        wosl = [p4.enter_context(nc.sbuf_tensor(f"p4wo{i}", [128, NK, 256], BF16)) for i in range(2)]
        bosl = [p4.enter_context(nc.sbuf_tensor(f"p4bo{i}", [128, 256], F32)) for i in range(2)]
        sch = [p4.enter_context(nc.sbuf_tensor(f"p4sc{i}", [128, 256], BF16)) for i in range(4)]
        mask_sb = p4.enter_context(nc.sbuf_tensor("p4m", [128, 4, 512], BF16))
        mx_sb = p4.enter_context(nc.sbuf_tensor("p4mx", [128, 4], F32))
        ngm = [p4.enter_context(nc.sbuf_tensor(f"p4ng{i}", [128, 1], F32)) for i in range(2)]
        ssl = [p4.enter_context(nc.sbuf_tensor(f"p4sl{i}", [128, 4], F32)) for i in range(2)]
        ssum = [p4.enter_context(nc.sbuf_tensor(f"p4su{i}", [128, 1], F32)) for i in range(2)]
        rp = [p4.enter_context(nc.sbuf_tensor(f"p4rp{i}", [128, 1], F32)) for i in range(2)]
        scp = p4.enter_context(nc.psum_tensor("p4sc", [128, 4, 512], F32))
        pTp = p4.enter_context(nc.psum_tensor("p4tp", [128, 2, 4, 128], F32))
        ctxp = p4.enter_context(nc.psum_tensor("p4cp", [128, 512], F32))
        wop = p4.enter_context(nc.psum_tensor("p4wp", [128, 256], F32))

        def f_ld4():
            def f(sy):
                sy.wait_ge(sSc.h, p3_sc_end)
                sQ.inc(sy.dma_start(out=mask_sb[:], in_=mask_d))
                sQ.inc(sy.dma_start(out=k_sb[:], in_=k_sp.rearrange("(g p) t -> p g t", p=128)))
            return f
        block.sync(f_ld4())
        k_ready = sQ.v

        vt_load = [0]
        vt_free = [0]
        vT_ev = [0, 0]
        for g in range(G):
            def f_vl(g=g):
                def f(sy):
                    if g >= 1:
                        sy.wait_ge(sPE.h, vt_free[0])
                    sQ.inc(sy.dma_start(out=vt_b[:], in_=v_sp[g * 128:(g + 1) * 128, :]))
                    vt_load[0] = sQ.v
                return f
            block.sync(f_vl())
            for grp in range(4):
                tb = grp % 2

                def f_pe(g=g, grp=grp, tb=tb):
                    def f(t):
                        if grp == 0:
                            t.wait_ge(sQ.h, vt_load[0])
                        t.wait_ge(sV.h, vT_ev[tb])
                        for j in range(4):
                            jt = grp * 4 + j
                            ins = t.matmul(pTp[:, tb, j, :], vt_b[:, jt * 128:(jt + 1) * 128],
                                           ident_sb[:], start=True, stop=True)
                        sPE.inc(ins)
                    return f
                block.tensor(f_pe())
                pec = sPE.v

                def f_ev(g=g, grp=grp, tb=tb, pec=pec):
                    def f(v):
                        v.wait_ge(sPE.h, pec)
                        vT_ev[tb] = sV.inc(v.tensor_copy(out=v_sb[:, grp * 4:(grp + 1) * 4, g, :],
                                                         in_=pTp[:, tb, :, :]))
                    return f
                block.vector(f_ev())
            vt_free[0] = sPE.v
        v_ready_ev = max(vT_ev)

        bank_exp = [0, 0, 0, 0]
        pT_ev = [0, 0]
        ctx_ev = [0]
        ctx_read = [0]
        q_load = {}
        q_free = [0, 0]
        pn_free = [0, 0]
        wo_ev = [0]
        wosl_used = [0, 0]
        sch_wr = [0, 0, 0, 0]
        wo_load = [0, 0]
        ctxT_last_ev = [0, 0]
        exp_defer = []

        def emit_wo_subunit(pislab, su):
            cpar = pislab % 2
            u = su // 4
            it = su % 4
            par = u % 2

            if it == 0:
                def f_wl(pislab=pislab, u=u, par=par):
                    def f(sy):
                        sy.wait_ge(sPE.h, wosl_used[par])
                        if pislab == 0 and u == 0:
                            sy.wait_ge(sG.h, wo_pre)
                        sQ.inc(sy.dma_start(
                            out=wosl[par][:],
                            in_=wob_sp[:, u * 256:(u + 1) * 256].rearrange("(k p) c -> p k c", p=128)))
                        sQ.inc(sy.dma_start(out=bosl[par][:], in_=bo_d[:, u * 256:(u + 1) * 256]))
                        wo_load[par] = sQ.v
                    return f
                block.sync(f_wl())

            def f_wo(pislab=pislab, u=u, it=it, par=par, cpar=cpar):
                def f(t):
                    if u == 0 and it == 0:
                        t.wait_ge(sV.h, ctxT_last_ev[cpar])
                    t.wait_ge(sQ.h, wo_load[par])
                    t.wait_ge(sV.h, wo_ev[0])
                    for dq in range(NK):
                        ins = t.matmul(wop[:], ctxT[:, cpar, dq, it * 128:(it + 1) * 128],
                                       wosl[par][:, dq, :], start=(dq == 0), stop=(dq == NK - 1))
                    sPE.inc(ins)
                return f
            block.tensor(f_wo())
            woc = sPE.v

            sb = su % 4

            def f_sev(it=it, par=par, sb=sb, woc=woc):
                def f(v):
                    v.wait_ge(sPE.h, woc)
                    v.wait_ge(sQ.h, sch_wr[sb])
                    wo_ev[0] = sV.inc(v.tensor_tensor(out=sch[sb][:], in0=wop[:],
                                                      in1=bosl[par][:], op=ALU.add))
                return f
            block.vector(f_sev())
            sevc = sV.v

            def f_ssp(pislab=pislab, u=u, it=it, sb=sb, sevc=sevc):
                def f(sy):
                    sy.wait_ge(sV.h, sevc)
                    r0 = pislab * 512 + it * 128
                    sQ.inc(sy.dma_start(out=s_sp[r0:r0 + 128, u * 256:(u + 1) * 256],
                                        in_=sch[sb][:]), 16)
                    sch_wr[sb] = sQ.v
                return f
            block.sync(f_ssp())
            if it == 3:
                wosl_used[par] = sPE.v

        class WoFiller:

            def __init__(self, pislab):
                self.pislab = pislab
                self.cpar = pislab % 2
                self.su = 0
                self.dq = 0
                self.post = []
                self._load(0)

            def _load(self, u):
                par = u % 2

                def f_wl(pislab=self.pislab, u=u, par=par):
                    def f(sy):
                        sy.wait_ge(sPE.h, wosl_used[par])
                        if pislab == 0 and u == 0:
                            sy.wait_ge(sG.h, wo_pre)
                        sQ.inc(sy.dma_start(
                            out=wosl[par][:],
                            in_=wob_sp[:, u * 256:(u + 1) * 256].rearrange("(k p) c -> p k c", p=128)))
                        sQ.inc(sy.dma_start(out=bosl[par][:], in_=bo_d[:, u * 256:(u + 1) * 256]))
                        wo_load[par] = sQ.v
                    return f
                block.sync(f_wl())

            def pe_chunk(self, t, nmm):
                while nmm > 0 and self.su < 64:
                    u = self.su // 4
                    it = self.su % 4
                    par = u % 2
                    if self.dq == 0:
                        if u == 0 and it == 0:
                            t.wait_ge(sV.h, ctxT_last_ev[self.cpar])
                        t.wait_ge(sQ.h, wo_load[par])
                        t.wait_ge(sV.h, wo_ev[0])
                    n = min(nmm, NK - self.dq)
                    for dq in range(self.dq, self.dq + n):
                        ins = t.matmul(wop[:], ctxT[:, self.cpar, dq, it * 128:(it + 1) * 128],
                                       wosl[par][:, dq, :], start=(dq == 0), stop=(dq == NK - 1))
                    self.dq += n
                    nmm -= n
                    if self.dq == NK:
                        sPE.inc(ins)
                        self._finish(u, it, par, sPE.v)
                        self.dq = 0
                        self.su += 1

            def _finish(self, u, it, par, woc):
                sb = (u * 4 + it) % 4
                pislab = self.pislab

                def post():
                    def f_sev(v):
                        v.wait_ge(sPE.h, woc)
                        v.wait_ge(sQ.h, sch_wr[sb])
                        wo_ev[0] = sV.inc(v.tensor_tensor(out=sch[sb][:], in0=wop[:],
                                                          in1=bosl[par][:], op=ALU.add))
                    block.vector(f_sev)
                    sevc = sV.v

                    def f_ssp(sy):
                        sy.wait_ge(sV.h, sevc)
                        r0 = pislab * 512 + it * 128
                        sQ.inc(sy.dma_start(out=s_sp[r0:r0 + 128, u * 256:(u + 1) * 256],
                                            in_=sch[sb][:]), 16)
                        sch_wr[sb] = sQ.v
                    block.sync(f_ssp)
                    if it == 3:
                        wosl_used[par] = woc
                    if it == 0 and u + 1 < 16:
                        self._load(u + 1)
                self.post.append(post)

            def drain(self):
                for p_ in self.post:
                    p_()
                self.post = []

        def emit_head(islab, h, filler=None):
            ns = islab + 1
            par = h % 2
            cpar = islab % 2
            g = h // 4

            def f_ql(islab=islab, h=h, par=par):
                def f(sy):
                    sy.wait_ge(sPE.h, q_free[par])
                    sQ.inc(sy.dma_start(out=qb[par][:],
                                        in_=q_sp[h * 128:(h + 1) * 128,
                                                 islab * 512:(islab + 1) * 512]))
                    q_load[h] = sQ.v
                return f
            block.sync(f_ql())

            exp_cnt = [0, 0, 0, 0]
            ngm_cnt = [0, 0, 0, 0]
            nrm_cnt = [0, 0, 0, 0]

            def e_sc(it):
                def f(t):
                    if h == 0 and it == 0 and islab == 0:
                        t.wait_ge(sQ.h, k_ready)
                    t.wait_ge(sQ.h, q_load[h])
                    for js in range(ns):
                        t.wait_ge(sA.h, bank_exp[js])
                        ins = t.matmul(scp[:, js, :], qb[par][:, it * 128:(it + 1) * 128],
                                       k_sb[:, g, js * 512:(js + 1) * 512],
                                       start=True, stop=(js < ns - 1))
                        if js == ns - 1:
                            ins = t.matmul(scp[:, js, :], ident_sb[:],
                                           mask_sb[:, it, :], start=False, stop=True)
                        sPE.inc(ins)
                return f

            def e_mx(it, sc_cnt):
                ipar = it % 2

                def f(v):
                    for js in range(ns):
                        v.wait_ge(sPE.h, sc_cnt - ns + 1 + js)
                        sV.inc(v.tensor_reduce(out=mx_sb[:, js:js + 1], in_=scp[:, js, :],
                                               axis=AX.X, op=ALU.max))
                    v.wait_ge(sV.h, sV.v)
                    sV.inc(v.tensor_reduce(out=ngm[ipar][:], in_=mx_sb[:, 0:ns],
                                           axis=AX.X, op=ALU.max, negate=True))
                return f

            def e_exp(it):
                ipar = it % 2

                def f(s, ng=None, pf=None):
                    s.wait_ge(sV.h, ng)
                    s.wait_ge(sPE.h, pf)
                    for js in range(ns):
                        ins = s.activation(out=pn[ipar][:, js * 512:(js + 1) * 512],
                                           in_=scp[:, js, :], func=AF.Exp,
                                           bias=ngm[ipar][:],
                                           accum_out=ssl[ipar][:, js:js + 1])
                        sA.inc_planned(ins)
                return f

            def e_nrm(it):
                ipar = it % 2

                def f(v):
                    v.wait_ge(sA.h, exp_cnt[it])
                    sV.inc(v.tensor_reduce(out=ssum[ipar][:], in_=ssl[ipar][:, 0:ns],
                                           axis=AX.X, op=ALU.add))
                    v.wait_ge(sV.h, sV.v)
                    sV.inc(v.reciprocal(out=rp[ipar][:], in_=ssum[ipar][:]))
                    v.wait_ge(sV.h, sV.v)
                    sV.inc(v.tensor_scalar(out=pn[ipar][:, 0:ns * 512],
                                           in0=pn[ipar][:, 0:ns * 512],
                                           scalar1=rp[ipar][:], scalar2=None, op0=ALU.mult))
                return f

            def e_T(it):
                ipar = it % 2

                def mk(bq4, tb):
                    def f(t):
                        if bq4 == 0:
                            t.wait_ge(sV.h, nrm_cnt[it])
                            t.wait_ge(sPE.h, ctx_read[0])
                        t.wait_ge(sV.h, pT_ev[tb])
                        for j in range(4):
                            jt = bq4 * 4 + j
                            ins = t.matmul(pTp[:, tb, j, :],
                                           pn[ipar][:, jt * 128:(jt + 1) * 128],
                                           ident_sb[:], start=True, stop=True)
                        sPE.inc(ins)
                    return f
                return mk

            order = []
            for it in range(4):
                order.append(("sc", it))
                order.append(("mx", it))
                order.append(("exp", it))
                if it >= 1:
                    order.append(("T", it - 1))
                    order.append(("ev", it - 1))
                order.append(("nrm", it))
            order += [("T", 3), ("ev", 3)]

            for kind, it in order:
                if kind == "sc":
                    block.tensor(e_sc(it))
                    sc_cnt = sPE.v
                    if filler is not None:
                        block.tensor(lambda t: filler.pe_chunk(t, 16))
                        filler.drain()
                    block.vector(e_mx(it, sc_cnt))
                    ngm_cnt[it] = sV.v
                elif kind == "mx":
                    pass
                elif kind == "exp":
                    base = sA.v
                    for js in range(ns):
                        bank_exp[js] = sA.plan(1)
                    fimpl = e_exp(it)
                    exp_defer.append(lambda s, fimpl=fimpl, ng=ngm_cnt[it],
                                     pf=pn_free[it % 2]: fimpl(s, ng=ng, pf=pf))
                    exp_cnt[it] = sA.v
                elif kind == "nrm":
                    block.vector(e_nrm(it))
                    nrm_cnt[it] = sV.v
                elif kind == "T":
                    mk = e_T(it)
                    for bq4 in range(ns):
                        tb = (it * 4 + bq4) % 2
                        block.tensor(mk(bq4, tb))
                        pec = sPE.v

                        def f_pev(it=it, bq4=bq4, tb=tb, pec=pec):
                            def f(v):
                                v.wait_ge(sPE.h, pec)
                                pT_ev[tb] = sV.inc(v.tensor_copy(
                                    out=pTs[:, bq4 * 4:(bq4 + 1) * 4, it * 128:(it + 1) * 128],
                                    in_=pTp[:, tb, :, :]))
                            return f
                        block.vector(f_pev())
                    pn_free[it % 2] = sPE.v
                elif kind == "ev":
                    pass

            pT_all = max(pT_ev)

            def f_ctx(islab=islab, h=h, g=g, ns=ns, cpar=cpar, pT_all=pT_all):
                def f(t):
                    t.wait_ge(sV.h, pT_all)
                    if h == 0 and islab == 0:
                        t.wait_ge(sV.h, v_ready_ev)
                    t.wait_ge(sV.h, ctx_ev[0])
                    for jt in range(4 * ns):
                        ins = t.matmul(ctxp[:], v_sb[:, jt, g, :], pTs[:, jt, :],
                                       start=(jt == 0), stop=(jt == 4 * ns - 1))
                    sPE.inc(ins)
                    ctx_read[0] = sPE.v
                return f
            block.tensor(f_ctx())
            ctx_cnt = sPE.v
            q_free[par] = sPE.v

            def f_cev(islab=islab, h=h, cpar=cpar, ctx_cnt=ctx_cnt):
                def f(v):
                    v.wait_ge(sPE.h, ctx_cnt)
                    ctx_ev[0] = sV.inc(v.tensor_copy(out=ctxT[:, cpar, h, :], in_=ctxp[:]))
                    ctxT_last_ev[cpar] = ctx_ev[0]
                return f
            block.vector(f_cev())

        for islab in range(4):
            exp_defer = []
            filler = WoFiller(islab - 1) if islab >= 1 else None
            for h in range(H):
                emit_head(islab, h, filler)
            if filler is not None:
                assert filler.su == 64 and not filler.post, (filler.su, len(filler.post))

            def f_exp_flush(fns=tuple(exp_defer)):
                def fn(s):
                    for g_ in fns:
                        g_(s)
                return fn
            block.scalar(f_exp_flush())
        for su in range(64):
            emit_wo_subunit(3, su)
        barrier()
        p4.close()
        p4_sc_end = sSc.v

    if nphases >= 6:
        p5 = contextlib.ExitStack()
        h_sl = p5.enter_context(nc.sbuf_tensor("p5h", [128, NFF, 512], BF16))
        dwb = [p5.enter_context(nc.sbuf_tensor(f"p5w{i}", [128, 11, 512], BF16)) for i in range(3)]
        ds_sb = p5.enter_context(nc.sbuf_tensor("p5ds", [128, DM], F32))
        t3 = [p5.enter_context(nc.sbuf_tensor(f"p5t3{i}", [128, 512], F32)) for i in range(2)]
        gsb = [p5.enter_context(nc.sbuf_tensor(f"p5g{i}", [128, 512], BF16)) for i in range(2)]
        xb5 = [p5.enter_context(nc.sbuf_tensor(f"p5x{i}", [128, 512], F32)) for i in range(2)]
        sb5 = [p5.enter_context(nc.sbuf_tensor(f"p5s{i}", [128, 512], BF16)) for i in range(2)]
        t4 = [p5.enter_context(nc.sbuf_tensor(f"p5t4{i}", [128, 512], F32)) for i in range(2)]
        ob5 = [p5.enter_context(nc.sbuf_tensor(f"p5o{i}", [128, 512], F32)) for i in range(2)]
        dps = p5.enter_context(nc.psum_tensor("p5p", [128, 4, 512], F32))

        def f_ld5():
            def f(sy):
                sQ.inc(sy.dma_start(out=ds_sb[:], in_=ds_d))
            return f
        block.sync(f_ld5())
        ds_ready = sQ.v

        hload = [0]
        h_used = [0]
        dwload = [0, 0, 0]
        dw_used = [0, 0, 0]
        bank_ev = [0, 0, 0, 0]
        t3_used = [0, 0]
        g_used = [0, 0]
        o_done = [0, 0]
        o_wr = [0, 0]
        t4_used = [0, 0]

        NFG = 8
        fgrp = [(a * 11, 11 if a < 7 else NFF - 77) for a in range(NFG)]

        def emit_dwl(ts, dsb, fg):
            buf = (dsb * NFG + fg) % 3

            def f(sy):
                sy.wait_ge(sPE.h, dw_used[buf])
                f0, nf = fgrp[fg]
                r0 = (dsb * NFF + f0) * 128
                sQ.inc(sy.dma_start(out=dwb[buf][:, 0:nf, :],
                                    in_=dwT_sp[r0:r0 + nf * 128, :].rearrange(
                                        "(f p) d -> p f d", p=128)))
                dwload[buf] = sQ.v
            block.sync(f)
            return sQ.v

        for ts in range(4):
            def f_hl(ts=ts):
                def f(s):
                    if ts == 0:
                        s.wait_ge(sSc.h, p4_sc_end)
                    s.wait_ge(sPE.h, h_used[0])
                    for fg in range(NFG):
                        f0, nf = fgrp[fg]
                        src = h_sp[f0 * 128:(f0 + nf) * 128, ts * 512:(ts + 1) * 512]
                        sSc.inc(s.dma_start(out=h_sl[:, f0:f0 + nf, :],
                                            in_=src.rearrange("(f p) t -> p f t", p=128)), 16)
                    hload[0] = sSc.v
                return f
            h_used[0] = sPE.v
            block.scalar(f_hl())

            dwl_cnt = {}
            dwl_cnt[(0, 0)] = emit_dwl(ts, 0, 0)
            dwl_cnt[(0, 1)] = emit_dwl(ts, 0, 1)

            for dsb in range(8):
                for fg in range(NFG):
                    nxt = dsb * NFG + fg + 2
                    if nxt < 8 * NFG:
                        dwl_cnt[(nxt // NFG, nxt % NFG)] = emit_dwl(ts, nxt // NFG, nxt % NFG)

                    def f_mm(ts=ts, dsb=dsb, fg=fg, dc=dwl_cnt[(dsb, fg)]):
                        def f(t):
                            buf = (dsb * NFG + fg) % 3
                            t.wait_ge(sQ.h, dc)
                            if fg == 0:
                                t.wait_ge(sSc.h, hload[0])
                                for it in range(4):
                                    t.wait_ge(sV.h, bank_ev[it])
                            f0, nf = fgrp[fg]
                            for fi in range(nf):
                                f_ = f0 + fi
                                for it in range(4):
                                    ins = t.matmul(dps[:, it, :], h_sl[:, f_, it * 128:(it + 1) * 128],
                                                   dwb[buf][:, fi, :],
                                                   start=(f_ == 0), stop=(f_ == NFF - 1))
                            sPE.inc(ins)
                            dw_used[buf] = sPE.v
                        return f
                    block.tensor(f_mm())
                mm_cnt = sPE.v

                si_defer = []
                for it in range(4):
                    par = it % 2

                    def f_t3(ts=ts, dsb=dsb, it=it, par=par, mm_cnt=mm_cnt):
                        def f(v):
                            v.wait_ge(sPE.h, mm_cnt)
                            if ts == 0 and dsb == 0 and it == 0:
                                v.wait_ge(sQ.h, ds_ready)
                            v.wait_ge(sA.h, t3_used[par])
                            bank_ev[it] = sV.inc(v.tensor_tensor(
                                out=t3[par][:], in0=dps[:, it, :],
                                in1=ds_sb[:, dsb * 512:(dsb + 1) * 512], op=ALU.mult))
                        return f
                    block.vector(f_t3())
                    t3c = sV.v

                    sic = sA.plan(1)
                    t3_used[par] = sic

                    def d_si(par=par, t3c=t3c, gu=g_used[par]):
                        def f(s):
                            s.wait_ge(sV.h, t3c)
                            s.wait_ge(sV.h, gu)
                            ins = s.activation(out=gsb[par][:], in_=t3[par][:], func=AF.Silu)
                            sA.inc_planned(ins)
                        return f
                    si_defer.append(d_si())

                    def f_xl(ts=ts, dsb=dsb, it=it, par=par):
                        def f(g):
                            if ts == 0 and dsb == 0 and it == 0:
                                g.wait_ge(sSc.h, p4_sc_end)
                            g.wait_ge(sV.h, o_done[par])
                            r0 = ts * 512 + it * 128
                            sG.inc(g.dma_start(out=xb5[par][:],
                                               in_=x_d[r0:r0 + 128, dsb * 512:(dsb + 1) * 512]), 16)
                            sG.inc(g.dma_start(out=sb5[par][:],
                                               in_=s_sp[r0:r0 + 128, dsb * 512:(dsb + 1) * 512]), 16)
                        return f
                    block.gpsimd(f_xl())
                    xlc = sG.v

                    def f_fin(it=it, par=par, sic=sic, xlc=xlc):
                        def f(v):
                            v.wait_ge(sA.h, sic)
                            v.wait_ge(sG.h, xlc)
                            v.wait_ge(sV.h, t4_used[par])
                            ins = v.tensor_tensor(out=t4[par][:], in0=sb5[par][:], in1=gsb[par][:],
                                                  op=ALU.mult)
                            g_used[par] = sV.inc(ins)
                            v.wait_ge(sV.h, sV.v)
                            v.wait_ge(sG.h, o_wr[par])
                            ins = v.tensor_tensor(out=ob5[par][:], in0=t4[par][:], in1=xb5[par][:],
                                                  op=ALU.add)
                            o_done[par] = sV.inc(ins)
                            t4_used[par] = sV.v
                        return f
                    block.vector(f_fin())
                    fic = sV.v

                    def f_ow(ts=ts, dsb=dsb, it=it, par=par, fic=fic):
                        def f(g):
                            g.wait_ge(sV.h, fic)
                            r0 = ts * 512 + it * 128
                            sG.inc(g.dma_start(out=out_d[r0:r0 + 128, dsb * 512:(dsb + 1) * 512],
                                               in_=ob5[par][:]), 16)
                            o_wr[par] = sG.v
                        return f
                    block.gpsimd(f_ow())

                def f_si_flush(fns=tuple(si_defer)):
                    def fn(s):
                        for g_ in fns:
                            g_(s)
                    return fn
                block.scalar(f_si_flush())
            h_used[0] = sPE.v

        def f_fin_wait():
            def f(g):
                g.wait_ge(sG.h, sG.v)
            return f
        block.gpsimd(f_fin_wait())

        def f_fin_wait2():
            def f(sy):
                sy.wait_ge(sG.h, sG.v)
            return f
        block.sync(f_fin_wait2())
        p5.close()

    block_cm.__exit__(None, None, None)
    xn_es.close()
    es.close()
    return nc


def make_consts():
    inv = 1.0 / (10000.0 ** (np.arange(64, dtype=np.float64) / 64.0))
    pos = np.arange(T, dtype=np.float64)
    ang = pos[None, :] * inv[:, None]
    cos = np.cos(ang)
    sin = np.sin(ang)
    qs = 1.0 / np.sqrt(128.0)
    qcos = np.concatenate([cos, cos], 0) * qs
    qsin = np.concatenate([-sin, sin], 0) * qs
    kcos = np.concatenate([cos, cos], 0)
    ksin = np.concatenate([-sin, sin], 0)
    masks = np.zeros((128, 4, 512), np.float32)
    for r in range(4):
        i = np.arange(128)[:, None]
        jj = np.arange(512)[None, :]
        masks[:, r, :] = np.where(jj <= 128 * r + i, 0.0, NEG)
    ident = np.eye(128, dtype=ml_dtypes.bfloat16)
    return (qcos.astype(np.float32), qsin.astype(np.float32),
            kcos.astype(np.float32), ksin.astype(np.float32),
            masks.astype(ml_dtypes.bfloat16), ident)


def make_in_map(inputs, b, consts):
    qcos, qsin, kcos, ksin, masks, ident = consts
    A = np.ascontiguousarray
    x = np.asarray(inputs["L0_attn_hidden_states"])
    d = {
        "x": A(x[b].astype(np.float32, copy=False)),
        "wq": A(np.asarray(inputs["L0_attn_Wq"], np.float32)),
        "wk": A(np.asarray(inputs["L0_attn_Wk"], np.float32)),
        "wv": A(np.asarray(inputs["L0_attn_Wv"], np.float32)),
        "wo": A(np.asarray(inputs["L0_attn_Wo"], np.float32)),
        "bq2": A(np.asarray(inputs["L0_attn_bq"], np.float32).reshape(H, 128).T),
        "bk2": A(np.asarray(inputs["L0_attn_bk"], np.float32).reshape(G, 128).T),
        "bv2": A(np.asarray(inputs["L0_attn_bv"], np.float32).reshape(G, 128).T),
        "bo_b": A(np.broadcast_to(np.asarray(inputs["L0_attn_bo"], np.float32), (128, DM))),
        "gw": A(np.asarray(inputs["L0_ffn_gate_w"], np.int32)),
        "uw": A(np.asarray(inputs["L0_ffn_up_w"], np.int32)),
        "dw": A(np.asarray(inputs["L0_ffn_down_w"], np.int32)),
        "gs2": A(np.asarray(inputs["L0_ffn_gate_s"], np.float32).reshape(NFF, 128).T),
        "us2": A(np.asarray(inputs["L0_ffn_up_s"], np.float32).reshape(NFF, 128).T),
        "ds_b": A(np.broadcast_to(np.asarray(inputs["L0_ffn_down_s"], np.float32).reshape(DM), (128, DM))),
        "ident": ident,
        "qcos": A(qcos), "qsin": A(qsin), "kcos": A(kcos), "ksin": A(ksin),
        "masks": A(masks),
    }
    return d


def kernel(**inputs):
    consts = make_consts()
    nc = build_program()
    in_maps = [make_in_map(inputs, b, consts) for b in range(8)]
    res = run_bass_kernel_spmd(nc, in_maps, core_ids=list(range(8)))
    out = np.stack([np.asarray(res.results[c]["out"], np.float32) for c in range(8)], axis=0)
    return out


# version 13
# speedup vs baseline: 1.0900x; 1.0900x over previous
import contextlib
import numpy as np
import ml_dtypes
import concourse.bass as bass
import concourse.mybir as mybir
from concourse.bass_utils import run_bass_kernel_spmd

F32 = mybir.dt.float32
BF16 = mybir.dt.bfloat16
I32 = mybir.dt.int32
AF = mybir.ActivationFunctionType
ALU = mybir.AluOpType
AX = mybir.AxisListType

T = 2048
DM = 4096
NK = DM // 128
DQ = 4096
DKV = 1024
H = 32
G = 8
D = 128
DFF = 11008
NFF = DFF // 128
NT = T // 128
NSL = T // 512
EPS = 1e-5
NEG = -1e30


class Sem:
    __slots__ = ("h", "v")

    def __init__(self, h):
        self.h = h
        self.v = 0

    def inc(self, instr, n=1):
        if "DMA" in type(instr.ins).__name__ and n < 16:
            n = 16
        instr.then_inc(self.h, n)
        self.v += n
        return self.v

    def plan(self, n=1):
        self.v += n
        return self.v

    def inc_planned(self, instr, n=1):
        instr.then_inc(self.h, n)


def build_program(nphases=6, dbg=()):
    nc = bass.Bass("TRN2", target_bir_lowering=False)
    es = contextlib.ExitStack()

    def dram(name, shape, dtype, kind):
        if kind == "Internal" and name in dbg:
            kind = "ExternalOutput"
        return nc.dram_tensor(name, shape, dtype, kind=kind).ap()

    x_d = dram("x", [T, DM], F32, "ExternalInput")
    wq_d = dram("wq", [DM, DQ], F32, "ExternalInput")
    wk_d = dram("wk", [DM, DKV], F32, "ExternalInput")
    wv_d = dram("wv", [DM, DKV], F32, "ExternalInput")
    wo_d = dram("wo", [DQ, DM], F32, "ExternalInput")
    bq_d = dram("bq2", [128, H], F32, "ExternalInput")
    bk_d = dram("bk2", [128, G], F32, "ExternalInput")
    bv_d = dram("bv2", [128, G], F32, "ExternalInput")
    bo_d = dram("bo_b", [128, DM], F32, "ExternalInput")
    gw_d = dram("gw", [DFF, DM], I32, "ExternalInput")
    uw_d = dram("uw", [DFF, DM], I32, "ExternalInput")
    dw_d = dram("dw", [DM, DFF], I32, "ExternalInput")
    gs_d = dram("gs2", [128, NFF], F32, "ExternalInput")
    us_d = dram("us2", [128, NFF], F32, "ExternalInput")
    ds_d = dram("ds_b", [128, DM], F32, "ExternalInput")
    id_d = dram("ident", [128, 128], BF16, "ExternalInput")
    qcos_d = dram("qcos", [128, T], F32, "ExternalInput")
    qsin_d = dram("qsin", [128, T], F32, "ExternalInput")
    kcos_d = dram("kcos", [128, T], F32, "ExternalInput")
    ksin_d = dram("ksin", [128, T], F32, "ExternalInput")
    mask_d = dram("masks", [128, 4, 512], BF16, "ExternalInput")

    out_d = dram("out", [T, DM], F32, "ExternalOutput")

    q_sp = dram("q_sp", [H * 128, T], BF16, "Internal")
    k_sp = dram("k_sp", [G * 128, T], BF16, "Internal")
    v_sp = dram("v_sp", [G * 128, T], BF16, "Internal")
    h_sp = dram("h_sp", [NFF * 128, T], BF16, "Internal")
    dwT_sp = dram("dwT_sp", [8 * NFF * 128, 512], BF16, "Internal")
    wob_sp = dram("wob_sp", [DQ, DM], BF16, "Internal")
    s_sp = dram("s_sp", [T, DM], BF16, "Internal")

    sQ = Sem(es.enter_context(nc.semaphore("sQ")))
    sSc = Sem(es.enter_context(nc.semaphore("sSc")))
    sG = Sem(es.enter_context(nc.semaphore("sG")))
    sPE = Sem(es.enter_context(nc.semaphore("sPE")))
    sV = Sem(es.enter_context(nc.semaphore("sV")))
    sA = Sem(es.enter_context(nc.semaphore("sA")))
    sBar = Sem(es.enter_context(nc.semaphore("sBar")))

    ident_sb = es.enter_context(nc.sbuf_tensor("ident_sb", [128, 128], BF16))
    xn_es = contextlib.ExitStack()
    xnT = xn_es.enter_context(nc.sbuf_tensor("xnT", [128, NK, T], BF16))

    block_cm = nc.Block()
    block = block_cm.__enter__()

    ENG = ["tensor", "vector", "scalar", "gpsimd", "sync"]

    def barrier():
        for e in ENG:
            def mk(e=e):
                def f(eng):
                    sBar.inc(eng.nop())
                return f
            getattr(block, e)(mk())
        tgt = sBar.v
        for e in ENG:
            def mk2(e=e):
                def f(eng):
                    eng.wait_ge(sBar.h, tgt)
                return f
            getattr(block, e)(mk2())

    def f_const(sy):
        sQ.inc(sy.dma_start(out=ident_sb[:], in_=id_d))
    block.sync(f_const)

    if nphases >= 1:
        p0 = contextlib.ExitStack()
        x_b = [p0.enter_context(nc.sbuf_tensor(f"p0x{i}", [128, DM], F32)) for i in range(2)]
        xn_nat = [p0.enter_context(nc.sbuf_tensor(f"p0n{i}", [128, DM], BF16)) for i in range(2)]
        xsq = [p0.enter_context(nc.sbuf_tensor(f"p0q{i}", [128, DM], BF16)) for i in range(2)]
        rs = [p0.enter_context(nc.sbuf_tensor(f"p0rs{i}", [128, 1], F32)) for i in range(2)]
        nm = [p0.enter_context(nc.sbuf_tensor(f"p0nm{i}", [128, 1], F32)) for i in range(2)]
        sq = [p0.enter_context(nc.sbuf_tensor(f"p0sq{i}", [128, 1], F32)) for i in range(2)]
        lnv = [p0.enter_context(nc.sbuf_tensor(f"p0ln{i}", [128, 1], F32)) for i in range(2)]
        rstd = [p0.enter_context(nc.sbuf_tensor(f"p0rd{i}", [128, 1], F32)) for i in range(2)]
        eps_sb = p0.enter_context(nc.sbuf_tensor("p0eps", [128, 1], F32))
        tps = p0.enter_context(nc.psum_tensor("p0t", [128, 2, 4, 128], F32))

        def f_eps(v):
            sV.inc(v.memset(eps_sb[:], EPS))
        block.vector(f_eps)
        eps_ready = sV.v

        xload = [0, 0]
        xn_done = [0, 0]
        sq_done = [0, 0]
        pe_tile = [0, 0]
        ev_grp = [0, 0]
        sums_done = [0, 0]

        for i in range(NT):
            p = i % 2

            def f_sync(i=i, p=p):
                def f(sy):
                    if i >= 2:
                        sy.wait_ge(sV.h, xn_done[p])
                    sQ.inc(sy.dma_start(out=x_b[p][:], in_=x_d[i * 128:(i + 1) * 128, :]))
                    xload[p] = sQ.v
                return f
            block.sync(f_sync())

            def f_v1(i=i, p=p):
                def f(v):
                    v.wait_ge(sQ.h, xload[p])
                    if i >= 2:
                        v.wait_ge(sPE.h, pe_tile[p])
                    sV.inc(v.tensor_reduce(out=rs[p][:], in_=x_b[p][:], axis=AX.X, op=ALU.add))
                    v.wait_ge(sV.h, sV.v)
                    sV.inc(v.tensor_scalar(out=nm[p][:], in0=rs[p][:], scalar1=1.0 / DM,
                                           scalar2=None, op0=ALU.mult))
                    v.wait_ge(sV.h, sV.v)
                    sV.inc(v.tensor_scalar(out=x_b[p][:], in0=x_b[p][:], scalar1=nm[p][:],
                                           scalar2=None, op0=ALU.subtract))
                return f
            block.vector(f_v1())
            xc_done = sV.v

            def f_v1b(i=i, p=p, xc_done=xc_done):
                def f(v):
                    v.wait_ge(sV.h, xc_done)
                    sV.inc(v.tensor_tensor(out=xsq[p][:], in0=x_b[p][:], in1=x_b[p][:],
                                           op=ALU.mult))
                    v.wait_ge(sV.h, sV.v)
                    sV.inc(v.tensor_reduce(out=sq[p][:], in_=xsq[p][:], axis=AX.X, op=ALU.add))
                    sums_done[p] = sV.v
                return f
            block.vector(f_v1b())
            sumsq_done = sV.v

            def f_a1(i=i, p=p, sumsq_done=sumsq_done):
                def f(s):
                    s.wait_ge(sV.h, sumsq_done)
                    if i < 2:
                        s.wait_ge(sV.h, eps_ready)
                    sA.inc(s.activation(out=lnv[p][:], in_=sq[p][:], func=AF.Ln,
                                        scale=1.0 / DM, bias=eps_sb[:]))
                    s.wait_ge(sA.h, sA.v)
                    sA.inc(s.activation(out=rstd[p][:], in_=lnv[p][:], func=AF.Exp, scale=-0.5))
                return f
            block.scalar(f_a1())
            rstd_done = sA.v

            def f_v2(i=i, p=p, rstd_done=rstd_done):
                def f(v):
                    v.wait_ge(sA.h, rstd_done)
                    sV.inc(v.tensor_scalar(out=xn_nat[p][:], in0=x_b[p][:], scalar1=rstd[p][:],
                                           scalar2=None, op0=ALU.mult))
                    xn_done[p] = sV.v
                return f
            block.vector(f_v2())

            for grp in range(8):
                tb = grp % 2

                def f_pe(i=i, p=p, grp=grp, tb=tb):
                    def f(t):
                        if grp == 0:
                            t.wait_ge(sV.h, xn_done[p])
                            t.wait_ge(sQ.h, xload[p])
                        t.wait_ge(sV.h, ev_grp[tb])
                        for j in range(4):
                            c = grp * 4 + j
                            ins = t.matmul(tps[:, tb, j, :], xn_nat[p][:, c * 128:(c + 1) * 128],
                                           ident_sb[:], start=True, stop=True)
                        sPE.inc(ins)
                    return f
                block.tensor(f_pe())
                pe_cnt = sPE.v

                def f_ev(i=i, grp=grp, tb=tb, pe_cnt=pe_cnt):
                    def f(v):
                        v.wait_ge(sPE.h, pe_cnt)
                        ev_grp[tb] = sV.inc(v.tensor_copy(
                            out=xnT[:, grp * 4:(grp + 1) * 4, i * 128:(i + 1) * 128],
                            in_=tps[:, tb, :, :]))
                    return f
                block.vector(f_ev())
            pe_tile[p] = sPE.v
        barrier()
        p0.close()

    if nphases >= 2:
        p1 = contextlib.ExitStack()
        wsl = [p1.enter_context(nc.sbuf_tensor(f"p1w{i}", [128, NK, 128], BF16)) for i in range(2)]
        osb = [p1.enter_context(nc.sbuf_tensor(f"p1o{i}", [128, T], BF16)) for i in range(2)]
        swp = p1.enter_context(nc.sbuf_tensor("p1swp", [128, T], BF16))
        t1 = p1.enter_context(nc.sbuf_tensor("p1t1", [128, T], BF16))
        t2 = p1.enter_context(nc.sbuf_tensor("p1t2", [128, T], BF16))
        cos_sb = p1.enter_context(nc.sbuf_tensor("p1cos", [128, T], F32))
        sin_sb = p1.enter_context(nc.sbuf_tensor("p1sin", [128, T], F32))
        bq_sb = p1.enter_context(nc.sbuf_tensor("p1bq", [128, H], F32))
        bk_sb = p1.enter_context(nc.sbuf_tensor("p1bk", [128, G], F32))
        bv_sb = p1.enter_context(nc.sbuf_tensor("p1bv", [128, G], F32))
        qps = [p1.enter_context(nc.psum_tensor(f"p1p{i}", [128, 4, 512], F32)) for i in range(2)]

        specs = []
        for c in range(H):
            specs.append((wq_d, c, bq_sb, c, "q", q_sp, c))
        for g in range(G):
            specs.append((wk_d, g, bk_sb, g, "k", k_sp, g))
        for g in range(G):
            specs.append((wv_d, g, bv_sb, g, None, v_sp, g))

        def f_ld0():
            def f(sy):
                sQ.inc(sy.dma_start(out=bq_sb[:], in_=bq_d))
                sQ.inc(sy.dma_start(out=bk_sb[:], in_=bk_d))
                sQ.inc(sy.dma_start(out=bv_sb[:], in_=bv_d))
                sQ.inc(sy.dma_start(out=cos_sb[:], in_=qcos_d))
                sQ.inc(sy.dma_start(out=sin_sb[:], in_=qsin_d))
            return f
        block.sync(f_ld0())
        tab_ready = sQ.v
        bias_ready = sQ.v

        wload = [0, 0]
        mm_done = [0, 0]
        evac_done = [0, 0]
        spill_done = [0, 0]
        rope_last = [0]
        t2_done = [0]
        last_q_rope = [0]

        for j, (w_d, col, b_sb, bidx, rope, sp_d, spc) in enumerate(specs):
            p = j % 2

            def f_g(j=j, p=p, w_d=w_d, col=col):
                def f(g):
                    if j >= 2:
                        g.wait_ge(sPE.h, mm_done[p])
                    src = w_d[:, col * 128:(col + 1) * 128].rearrange("(k p) c -> p k c", p=128)
                    sG.inc(g.dma_start(out=wsl[p][:], in_=src), 16)
                    wload[p] = sG.v
                return f
            block.gpsimd(f_g())

            def f_pe(j=j, p=p):
                def f(t):
                    t.wait_ge(sG.h, wload[p])
                    if j >= 2:
                        t.wait_ge(sA.h, evac_done[p])
                    for k in range(NK):
                        for ts in range(4):
                            ins = t.matmul(qps[p][:, ts, :], wsl[p][:, k, :],
                                           xnT[:, k, ts * 512:(ts + 1) * 512],
                                           start=(k == 0), stop=(k == NK - 1))
                    sPE.inc(ins)
                    mm_done[p] = sPE.v
                return f
            block.tensor(f_pe())
            mm_cnt = sPE.v

            def f_a(j=j, p=p, b_sb=b_sb, bidx=bidx, mm_cnt=mm_cnt):
                def f(s):
                    s.wait_ge(sPE.h, mm_cnt)
                    if j == 0:
                        s.wait_ge(sQ.h, bias_ready)
                    if j >= 2:
                        s.wait_ge(sSc.h, spill_done[p])
                    for ts in range(4):
                        sA.inc(s.activation(out=osb[p][:, ts * 512:(ts + 1) * 512],
                                            in_=qps[p][:, ts, :], func=AF.Identity,
                                            bias=b_sb[:, bidx:bidx + 1]))
                    evac_done[p] = sA.v
                return f
            block.scalar(f_a())
            ev_cnt = sA.v

            if rope is not None:
                def f_sw(j=j, p=p, ev_cnt=ev_cnt):
                    def f(s):
                        s.wait_ge(sA.h, ev_cnt)
                        s.wait_ge(sV.h, t2_done[0])
                        sSc.inc(s.dma_start(out=swp[0:64, :], in_=osb[p][64:128, :]), 16)
                        sSc.inc(s.dma_start(out=swp[64:128, :], in_=osb[p][0:64, :]), 16)
                    return f
                block.scalar(f_sw())
                sw_cnt = sSc.v

                def f_rope(j=j, p=p, ev_cnt=ev_cnt, sw_cnt=sw_cnt, rope=rope):
                    def f(v):
                        v.wait_ge(sA.h, ev_cnt)
                        v.wait_ge(sSc.h, sw_cnt)
                        if j < 2 or j in (H, H + 1):
                            v.wait_ge(sQ.h, tab_ready)
                        v.wait_ge(sV.h, rope_last[0])
                        sV.inc(v.tensor_tensor(out=t1[:], in0=osb[p][:], in1=cos_sb[:], op=ALU.mult))
                        sV.inc(v.tensor_tensor(out=t2[:], in0=swp[:], in1=sin_sb[:], op=ALU.mult))
                        t2_done[0] = sV.v
                        v.wait_ge(sV.h, sV.v)
                        sV.inc(v.tensor_tensor(out=osb[p][:], in0=t1[:], in1=t2[:], op=ALU.add))
                        rope_last[0] = sV.v
                        if rope == "q":
                            last_q_rope[0] = sV.v
                    return f
                block.vector(f_rope())
                rdy = sV.v
            else:
                rdy = None

            def f_sp(j=j, p=p, sp_d=sp_d, spc=spc, rdy=rdy, ev_cnt=ev_cnt):
                def f(s):
                    if rdy is not None:
                        s.wait_ge(sV.h, rdy)
                    else:
                        s.wait_ge(sA.h, ev_cnt)
                    sSc.inc(s.dma_start(out=sp_d[spc * 128:(spc + 1) * 128, :], in_=osb[p][:]), 16)
                    spill_done[p] = sSc.v
                return f
            block.scalar(f_sp())

            if j == H - 1:
                def f_tb():
                    def f(sy):
                        sy.wait_ge(sV.h, last_q_rope[0])
                        sQ.inc(sy.dma_start(out=cos_sb[:], in_=kcos_d))
                        sQ.inc(sy.dma_start(out=sin_sb[:], in_=ksin_d))
                    return f
                block.sync(f_tb())
                tab_ready = sQ.v
        barrier()
        p1.close()

    if nphases >= 3:
        p2 = contextlib.ExitStack()
        nat = [p2.enter_context(nc.sbuf_tensor(f"p2n{i}", [128, DM], BF16)) for i in range(2)]
        gT = p2.enter_context(nc.sbuf_tensor("p2gT", [128, NK, 128], BF16))
        uT = p2.enter_context(nc.sbuf_tensor("p2uT", [128, NK, 128], BF16))
        t1g = p2.enter_context(nc.sbuf_tensor("p2t1", [128, T], BF16))
        t2u = [p2.enter_context(nc.sbuf_tensor(f"p2t2{i}", [128, 512], BF16)) for i in range(2)]
        hob = [p2.enter_context(nc.sbuf_tensor(f"p2h{i}", [128, 512], BF16)) for i in range(2)]
        gs_sb = p2.enter_context(nc.sbuf_tensor("p2gs", [128, NFF], F32))
        us_sb = p2.enter_context(nc.sbuf_tensor("p2us", [128, NFF], F32))
        tps2 = p2.enter_context(nc.psum_tensor("p2t", [128, 2, 4, 128], F32))
        gps = p2.enter_context(nc.psum_tensor("p2g", [128, 2, 512], F32))
        ups = p2.enter_context(nc.psum_tensor("p2u", [128, 2, 512], F32))

        def f_ld():
            def f(sy):
                sQ.inc(sy.dma_start(out=gs_sb[:], in_=gs_d))
                sQ.inc(sy.dma_start(out=us_sb[:], in_=us_d))
            return f
        block.sync(f_ld())
        gs_ready = sQ.v

        natload = [0, 0]
        natfree_pe = [0, 0]
        ev_tb = [0, 0]
        mmu_done = [0]
        mmg_done = [0]
        silu_done = [0, 0, 0, 0]
        t2_ts = [0, 0]
        h_ts = [0, 0]
        hsp_ts = [0, 0]
        gbank_free = [0, 0]
        ubank_free = [0, 0]

        def emit_T(src_par, dst_sb, prev_reader_cnt):
            evs = []
            for grp in range(8):
                tb = grp % 2

                def f_pe(grp=grp, tb=tb, src_par=src_par):
                    def fn(t):
                        if grp == 0:
                            t.wait_ge(sG.h, natload[src_par])
                        t.wait_ge(sV.h, ev_tb[tb])
                        for jj in range(4):
                            cc = grp * 4 + jj
                            ins = t.matmul(tps2[:, tb, jj, :], nat[src_par][:, cc * 128:(cc + 1) * 128],
                                           ident_sb[:], start=True, stop=True)
                        sPE.inc(ins)
                    return fn
                block.tensor(f_pe())
                pec = sPE.v

                def f_ev(grp=grp, tb=tb, dst_sb=dst_sb, pec=pec, prev=prev_reader_cnt):
                    def fn(v):
                        v.wait_ge(sPE.h, pec)
                        v.wait_ge(sPE.h, prev)
                        ev_tb[tb] = sV.inc(v.tensor_copy(
                            out=dst_sb[:, grp * 4:(grp + 1) * 4, :], in_=tps2[:, tb, :, :]))
                        evs.append(ev_tb[tb])
                    return fn
                block.vector(f_ev())
            return evs

        for f in range(NFF):
            for which, w_dd in ((0, gw_d), (1, uw_d)):
                def f_g(f=f, which=which, w_dd=w_dd):
                    def fn(g):
                        if f >= 1:
                            g.wait_ge(sPE.h, natfree_pe[which])
                        sG.inc(g.dma_start(out=nat[which][:],
                                           in_=w_dd[f * 128:(f + 1) * 128, :],
                                           max_dma_last_dim=8192), 16)
                        natload[which] = sG.v
                    return fn
                block.gpsimd(f_g())

            evs_g = emit_T(0, gT, mmg_done[0])
            natfree_pe[0] = sPE.v
            evs_u = emit_T(1, uT, mmu_done[0])
            natfree_pe[1] = sPE.v

            silu_defer = []
            for ts in range(4):
                pb = ts % 2

                def f_mmg(f=f, ts=ts, pb=pb, evg=max(evs_g)):
                    def fn(t):
                        if ts == 0:
                            t.wait_ge(sV.h, evg)
                        t.wait_ge(sA.h, gbank_free[pb])
                        for k in range(NK):
                            ins = t.matmul(gps[:, pb, :], gT[:, k, :],
                                           xnT[:, k, ts * 512:(ts + 1) * 512],
                                           start=(k == 0), stop=(k == NK - 1))
                        sPE.inc(ins)
                    return fn
                block.tensor(f_mmg())
                mmc = sPE.v

                planned = sA.plan(1)

                def d_silu(f=f, ts=ts, pb=pb, mmc=mmc, hfree=h_ts[ts % 2],
                           first=(f == 0 and ts == 0)):
                    def fn(s):
                        s.wait_ge(sPE.h, mmc)
                        if first:
                            s.wait_ge(sQ.h, gs_ready)
                        s.wait_ge(sV.h, hfree)
                        ins = s.activation(out=t1g[:, ts * 512:(ts + 1) * 512], in_=gps[:, pb, :],
                                           func=AF.Silu, scale=gs_sb[:, f:f + 1])
                        sA.inc_planned(ins)
                    return fn
                silu_defer.append(d_silu())
                gbank_free[pb] = planned
                silu_done[ts] = planned
            mmg_done[0] = sPE.v

            def f_silu_flush(fns=tuple(silu_defer)):
                def fn(s):
                    for g_ in fns:
                        g_(s)
                return fn
            block.scalar(f_silu_flush())

            for ts in range(4):
                pb = ts % 2

                def f_mmu(f=f, ts=ts, pb=pb, evu=max(evs_u)):
                    def fn(t):
                        if ts == 0:
                            t.wait_ge(sV.h, evu)
                        t.wait_ge(sV.h, ubank_free[pb])
                        for k in range(NK):
                            ins = t.matmul(ups[:, pb, :], uT[:, k, :],
                                           xnT[:, k, ts * 512:(ts + 1) * 512],
                                           start=(k == 0), stop=(k == NK - 1))
                        sPE.inc(ins)
                    return fn
                block.tensor(f_mmu())
                mmc = sPE.v

                def f_ep(f=f, ts=ts, pb=pb, mmc=mmc, sdone=silu_done[ts]):
                    def fn(v):
                        v.wait_ge(sPE.h, mmc)
                        if f == 0 and ts == 0:
                            v.wait_ge(sQ.h, gs_ready)
                        v.wait_ge(sSc.h, hsp_ts[ts % 2])
                        ins = v.tensor_scalar(out=t2u[ts % 2][:], in0=ups[:, pb, :],
                                              scalar1=us_sb[:, f:f + 1], scalar2=None, op0=ALU.mult)
                        ubank_free[pb] = sV.inc(ins)
                        t2_ts[ts % 2] = sV.v
                        v.wait_ge(sV.h, sV.v)
                        v.wait_ge(sA.h, sdone)
                        ins = v.tensor_tensor(out=hob[ts % 2][:], in0=t1g[:, ts * 512:(ts + 1) * 512],
                                              in1=t2u[ts % 2][:], op=ALU.mult)
                        h_ts[ts % 2] = sV.inc(ins)
                    return fn
                block.vector(f_ep())
                hcnt = sV.v

                def f_hsp(f=f, ts=ts, hcnt=hcnt):
                    def fn(s):
                        s.wait_ge(sV.h, hcnt)
                        sSc.inc(s.dma_start(out=h_sp[f * 128:(f + 1) * 128, ts * 512:(ts + 1) * 512],
                                            in_=hob[ts % 2][:]), 16)
                        hsp_ts[ts % 2] = sSc.v
                    return fn
                block.scalar(f_hsp())
            mmu_done[0] = sPE.v
        barrier()
        p2.close()
        xn_es.close()

    if nphases >= 4:
        p3 = contextlib.ExitStack()
        csl = [[p3.enter_context(nc.sbuf_tensor(f"p3c{s}_{j}", [128, DFF], BF16)) for j in range(4)]
               for s in range(2)]
        stg = [p3.enter_context(nc.sbuf_tensor(f"p3s{i}", [128, 8, 512], BF16)) for i in range(4)]
        tps3 = p3.enter_context(nc.psum_tensor("p3t", [128, 2, 4, 128], F32))

        def f_wo():
            def f(g):
                for i in range(8):
                    sG.inc(g.dma_start(out=wob_sp[i * 512:(i + 1) * 512, :],
                                       in_=wo_d[i * 512:(i + 1) * 512, :],
                                       max_dma_last_dim=8192), 16)
            return f
        block.gpsimd(f_wo())
        wo_pre = sG.v

        dsload = [0, 0]
        dsfree = [0, 0]
        ev3 = [0, 0]
        stg_wr = [0, 0, 0, 0]
        stg_ev = [0, 0, 0, 0]

        fgroups = [(a, min(8, NFF - a)) for a in range(0, NFF, 8)]

        for ds in range(8):
            sp = ds % 2

            def f_ld(ds=ds, sp=sp):
                def fn(g):
                    if ds >= 2:
                        g.wait_ge(sPE.h, dsfree[sp])
                    for j in range(4):
                        sG.inc(g.dma_start(out=csl[sp][j][:],
                                           in_=dw_d[(ds * 4 + j) * 128:(ds * 4 + j + 1) * 128, :],
                                           max_dma_last_dim=8192), 16)
                    dsload[sp] = sG.v
                return fn
            block.gpsimd(f_ld())

            for gi, (f0, nf) in enumerate(fgroups):
                bs = gi % 4

                for fi in range(nf):
                    f = f0 + fi
                    tb = f % 2

                    def f_pe(ds=ds, sp=sp, f=f, tb=tb, gi=gi, fi=fi, bs=bs):
                        def fn(t):
                            if f == 0:
                                t.wait_ge(sG.h, dsload[sp])
                            t.wait_ge(sV.h, ev3[tb])
                            for j in range(4):
                                ins = t.matmul(tps3[:, tb, j, :], csl[sp][j][:, f * 128:(f + 1) * 128],
                                               ident_sb[:], start=True, stop=True)
                            sPE.inc(ins)
                        return fn
                    block.tensor(f_pe())
                    pec = sPE.v

                    def f_ev(tb=tb, fi=fi, bs=bs, pec=pec):
                        def fn(v):
                            v.wait_ge(sPE.h, pec)
                            v.wait_ge(sSc.h, stg_wr[bs])
                            ev3[tb] = sV.inc(v.tensor_copy(out=stg[bs][:, fi, :],
                                                           in_=tps3[:, tb, :, :]))
                            stg_ev[bs] = ev3[tb]
                        return fn
                    block.vector(f_ev())

                def f_wr(ds=ds, f0=f0, nf=nf, bs=bs, ev=stg_ev, gi=gi):
                    cnt = stg_ev[bs]

                    def fn(s):
                        s.wait_ge(sV.h, cnt)
                        r0 = (ds * NFF + f0) * 128
                        dst = dwT_sp[r0:r0 + nf * 128, :].rearrange("(f p) d -> p f d", p=128)
                        sSc.inc(s.dma_start(out=dst, in_=stg[bs][:, 0:nf, :]), 16)
                        stg_wr[bs] = sSc.v
                    return fn
                block.scalar(f_wr())
            dsfree[sp] = sPE.v
        barrier()
        p3.close()
        p3_sc_end = sSc.v

    if nphases >= 5:
        p4 = contextlib.ExitStack()
        k_sb = p4.enter_context(nc.sbuf_tensor("p4k", [128, G, T], BF16))
        v_sb = p4.enter_context(nc.sbuf_tensor("p4v", [128, NT, G, 128], BF16))
        vt_b = p4.enter_context(nc.sbuf_tensor("p4vt", [128, T], BF16))
        qb = [p4.enter_context(nc.sbuf_tensor(f"p4q{i}", [128, 512], BF16)) for i in range(2)]
        pn = [p4.enter_context(nc.sbuf_tensor(f"p4pn{i}", [128, T], BF16)) for i in range(2)]
        pTs = p4.enter_context(nc.sbuf_tensor("p4pT", [128, 16, 512], BF16))
        ctxT = p4.enter_context(nc.sbuf_tensor("p4ctx", [128, 2, H, 512], BF16))
        wosl = [p4.enter_context(nc.sbuf_tensor(f"p4wo{i}", [128, NK, 256], BF16)) for i in range(2)]
        bosl = [p4.enter_context(nc.sbuf_tensor(f"p4bo{i}", [128, 256], F32)) for i in range(2)]
        sch = [p4.enter_context(nc.sbuf_tensor(f"p4sc{i}", [128, 256], BF16)) for i in range(4)]
        mask_sb = p4.enter_context(nc.sbuf_tensor("p4m", [128, 4, 512], BF16))
        mx_sb = p4.enter_context(nc.sbuf_tensor("p4mx", [128, 4], F32))
        ngm = [p4.enter_context(nc.sbuf_tensor(f"p4ng{i}", [128, 1], F32)) for i in range(2)]
        ssl = [p4.enter_context(nc.sbuf_tensor(f"p4sl{i}", [128, 4], F32)) for i in range(2)]
        ssum = [p4.enter_context(nc.sbuf_tensor(f"p4su{i}", [128, 1], F32)) for i in range(2)]
        rp = [p4.enter_context(nc.sbuf_tensor(f"p4rp{i}", [128, 1], F32)) for i in range(2)]
        scp = p4.enter_context(nc.psum_tensor("p4sc", [128, 4, 512], F32))
        pTp = p4.enter_context(nc.psum_tensor("p4tp", [128, 2, 4, 128], F32))
        ctxp = p4.enter_context(nc.psum_tensor("p4cp", [128, 512], F32))
        wop = p4.enter_context(nc.psum_tensor("p4wp", [128, 256], F32))

        def f_ld4():
            def f(sy):
                sy.wait_ge(sSc.h, p3_sc_end)
                sQ.inc(sy.dma_start(out=mask_sb[:], in_=mask_d))
                sQ.inc(sy.dma_start(out=k_sb[:], in_=k_sp.rearrange("(g p) t -> p g t", p=128)))
            return f
        block.sync(f_ld4())
        k_ready = sQ.v

        vt_load = [0]
        vt_free = [0]
        vT_ev = [0, 0]
        for g in range(G):
            def f_vl(g=g):
                def f(sy):
                    if g >= 1:
                        sy.wait_ge(sPE.h, vt_free[0])
                    sQ.inc(sy.dma_start(out=vt_b[:], in_=v_sp[g * 128:(g + 1) * 128, :]))
                    vt_load[0] = sQ.v
                return f
            block.sync(f_vl())
            for grp in range(4):
                tb = grp % 2

                def f_pe(g=g, grp=grp, tb=tb):
                    def f(t):
                        if grp == 0:
                            t.wait_ge(sQ.h, vt_load[0])
                        t.wait_ge(sV.h, vT_ev[tb])
                        for j in range(4):
                            jt = grp * 4 + j
                            ins = t.matmul(pTp[:, tb, j, :], vt_b[:, jt * 128:(jt + 1) * 128],
                                           ident_sb[:], start=True, stop=True)
                        sPE.inc(ins)
                    return f
                block.tensor(f_pe())
                pec = sPE.v

                def f_ev(g=g, grp=grp, tb=tb, pec=pec):
                    def f(v):
                        v.wait_ge(sPE.h, pec)
                        vT_ev[tb] = sV.inc(v.tensor_copy(out=v_sb[:, grp * 4:(grp + 1) * 4, g, :],
                                                         in_=pTp[:, tb, :, :]))
                    return f
                block.vector(f_ev())
            vt_free[0] = sPE.v
        v_ready_ev = max(vT_ev)

        rot = [0]
        bank_exp = [0, 0, 0, 0]
        pT_ev = [0, 0]
        ctx_ev = [0]
        ctx_read = [0]
        q_load = {}
        q_free = [0, 0]
        pn_free = [0, 0]
        wo_ev = [0]
        wo_ev2 = [0, 0]
        wosl_used = [0, 0]
        sch_wr = [0, 0, 0, 0]
        wo_load = [0, 0]
        ctxT_last_ev = [0, 0]
        exp_defer = []

        def emit_wo_subunit(pislab, su):
            cpar = pislab % 2
            u = su // 4
            it = su % 4
            par = u % 2

            if it == 0:
                def f_wl(pislab=pislab, u=u, par=par):
                    def f(sy):
                        sy.wait_ge(sPE.h, wosl_used[par])
                        if pislab == 0 and u == 0:
                            sy.wait_ge(sG.h, wo_pre)
                        sQ.inc(sy.dma_start(
                            out=wosl[par][:],
                            in_=wob_sp[:, u * 256:(u + 1) * 256].rearrange("(k p) c -> p k c", p=128)))
                        sQ.inc(sy.dma_start(out=bosl[par][:], in_=bo_d[:, u * 256:(u + 1) * 256]))
                        wo_load[par] = sQ.v
                    return f
                block.sync(f_wl())

            slot = su % 2
            wp = wop[:] if slot == 0 else ctxp[:, 0:256]

            def f_wo(pislab=pislab, u=u, it=it, par=par, cpar=cpar, slot=slot, wp=wp):
                def f(t):
                    if u == 0 and it == 0:
                        t.wait_ge(sV.h, ctxT_last_ev[cpar])
                        t.wait_ge(sV.h, ctx_ev[0])
                    t.wait_ge(sQ.h, wo_load[par])
                    t.wait_ge(sV.h, wo_ev2[slot])
                    for dq in range(NK):
                        ins = t.matmul(wp, ctxT[:, cpar, dq, it * 128:(it + 1) * 128],
                                       wosl[par][:, dq, :], start=(dq == 0), stop=(dq == NK - 1))
                    sPE.inc(ins)
                return f
            block.tensor(f_wo())
            woc = sPE.v

            sb = su % 4

            def f_sev(it=it, par=par, sb=sb, woc=woc, slot=slot, wp=wp):
                def f(v):
                    v.wait_ge(sPE.h, woc)
                    v.wait_ge(sQ.h, sch_wr[sb])
                    wo_ev2[slot] = sV.inc(v.tensor_tensor(out=sch[sb][:], in0=wp,
                                                          in1=bosl[par][:], op=ALU.add))
                    wo_ev[0] = wo_ev2[slot]
                return f
            block.vector(f_sev())
            sevc = sV.v

            def f_ssp(pislab=pislab, u=u, it=it, sb=sb, sevc=sevc):
                def f(sy):
                    sy.wait_ge(sV.h, sevc)
                    r0 = pislab * 512 + it * 128
                    sQ.inc(sy.dma_start(out=s_sp[r0:r0 + 128, u * 256:(u + 1) * 256],
                                        in_=sch[sb][:]), 16)
                    sch_wr[sb] = sQ.v
                return f
            block.sync(f_ssp())
            if it == 3:
                wosl_used[par] = sPE.v

        class WoFiller:

            def __init__(self, pislab):
                self.pislab = pislab
                self.cpar = pislab % 2
                self.su = 0
                self.dq = 0
                self.post = []
                self._load(0)

            def _load(self, u):
                par = u % 2

                def f_wl(pislab=self.pislab, u=u, par=par):
                    def f(sy):
                        sy.wait_ge(sPE.h, wosl_used[par])
                        if pislab == 0 and u == 0:
                            sy.wait_ge(sG.h, wo_pre)
                        sQ.inc(sy.dma_start(
                            out=wosl[par][:],
                            in_=wob_sp[:, u * 256:(u + 1) * 256].rearrange("(k p) c -> p k c", p=128)))
                        sQ.inc(sy.dma_start(out=bosl[par][:], in_=bo_d[:, u * 256:(u + 1) * 256]))
                        wo_load[par] = sQ.v
                    return f
                block.sync(f_wl())

            def pe_chunk(self, t, nmm):
                while nmm > 0 and self.su < 64:
                    u = self.su // 4
                    it = self.su % 4
                    par = u % 2
                    if self.dq == 0:
                        if u == 0 and it == 0:
                            t.wait_ge(sV.h, ctxT_last_ev[self.cpar])
                        t.wait_ge(sQ.h, wo_load[par])
                        t.wait_ge(sV.h, wo_ev[0])
                    n = min(nmm, NK - self.dq)
                    for dq in range(self.dq, self.dq + n):
                        ins = t.matmul(wop[:], ctxT[:, self.cpar, dq, it * 128:(it + 1) * 128],
                                       wosl[par][:, dq, :], start=(dq == 0), stop=(dq == NK - 1))
                    self.dq += n
                    nmm -= n
                    if self.dq == NK:
                        sPE.inc(ins)
                        self._finish(u, it, par, sPE.v)
                        self.dq = 0
                        self.su += 1

            def _finish(self, u, it, par, woc):
                sb = (u * 4 + it) % 4
                pislab = self.pislab

                def post():
                    def f_sev(v):
                        v.wait_ge(sPE.h, woc)
                        v.wait_ge(sQ.h, sch_wr[sb])
                        wo_ev[0] = sV.inc(v.tensor_tensor(out=sch[sb][:], in0=wop[:],
                                                          in1=bosl[par][:], op=ALU.add))
                    block.vector(f_sev)
                    sevc = sV.v

                    def f_ssp(sy):
                        sy.wait_ge(sV.h, sevc)
                        r0 = pislab * 512 + it * 128
                        sQ.inc(sy.dma_start(out=s_sp[r0:r0 + 128, u * 256:(u + 1) * 256],
                                            in_=sch[sb][:]), 16)
                        sch_wr[sb] = sQ.v
                    block.sync(f_ssp)
                    if it == 3:
                        wosl_used[par] = woc
                    if it == 0 and u + 1 < 16:
                        self._load(u + 1)
                self.post.append(post)

            def drain(self):
                for p_ in self.post:
                    p_()
                self.post = []

        def emit_head(islab, h, filler=None):
            ns = islab + 1
            par = h % 2
            cpar = islab % 2
            g = h // 4

            def f_ql(islab=islab, h=h, par=par):
                def f(sy):
                    sy.wait_ge(sPE.h, q_free[par])
                    sQ.inc(sy.dma_start(out=qb[par][:],
                                        in_=q_sp[h * 128:(h + 1) * 128,
                                                 islab * 512:(islab + 1) * 512]))
                    q_load[h] = sQ.v
                return f
            block.sync(f_ql())

            exp_cnt = [0, 0, 0, 0]
            ngm_cnt = [0, 0, 0, 0]
            nrm_cnt = [0, 0, 0, 0]
            base_it = [0, 0, 0, 0]

            def e_sc(it, base):
                def f(t):
                    if h == 0 and it == 0 and islab == 0:
                        t.wait_ge(sQ.h, k_ready)
                    t.wait_ge(sQ.h, q_load[h])
                    for js in range(ns):
                        t.wait_ge(sA.h, bank_exp[base + js])
                        ins = t.matmul(scp[:, base + js, :], qb[par][:, it * 128:(it + 1) * 128],
                                       k_sb[:, g, js * 512:(js + 1) * 512],
                                       start=True, stop=(js < ns - 1))
                        if js == ns - 1:
                            ins = t.matmul(scp[:, base + js, :], ident_sb[:],
                                           mask_sb[:, it, :], start=False, stop=True)
                        sPE.inc(ins)
                return f

            def e_mx(it, sc_cnt, base):
                ipar = it % 2

                def f(v):
                    for js in range(ns):
                        v.wait_ge(sPE.h, sc_cnt - ns + 1 + js)
                        sV.inc(v.tensor_reduce(out=mx_sb[:, js:js + 1], in_=scp[:, base + js, :],
                                               axis=AX.X, op=ALU.max))
                    v.wait_ge(sV.h, sV.v)
                    sV.inc(v.tensor_reduce(out=ngm[ipar][:], in_=mx_sb[:, 0:ns],
                                           axis=AX.X, op=ALU.max, negate=True))
                return f

            def e_exp(it, base):
                ipar = it % 2

                def f(s, ng=None, pf=None):
                    s.wait_ge(sV.h, ng)
                    s.wait_ge(sPE.h, pf)
                    for js in range(ns):
                        ins = s.activation(out=pn[ipar][:, js * 512:(js + 1) * 512],
                                           in_=scp[:, base + js, :], func=AF.Exp,
                                           bias=ngm[ipar][:],
                                           accum_out=ssl[ipar][:, js:js + 1])
                        sA.inc_planned(ins)
                return f

            def e_nrm(it):
                ipar = it % 2

                def f(v):
                    v.wait_ge(sA.h, exp_cnt[it])
                    sV.inc(v.tensor_reduce(out=ssum[ipar][:], in_=ssl[ipar][:, 0:ns],
                                           axis=AX.X, op=ALU.add))
                    v.wait_ge(sV.h, sV.v)
                    sV.inc(v.reciprocal(out=rp[ipar][:], in_=ssum[ipar][:]))
                    v.wait_ge(sV.h, sV.v)
                    sV.inc(v.tensor_scalar(out=pn[ipar][:, 0:ns * 512],
                                           in0=pn[ipar][:, 0:ns * 512],
                                           scalar1=rp[ipar][:], scalar2=None, op0=ALU.mult))
                return f

            def e_T(it):
                ipar = it % 2

                def mk(bq4, tb):
                    def f(t):
                        if bq4 == 0:
                            t.wait_ge(sV.h, nrm_cnt[it])
                            t.wait_ge(sPE.h, ctx_read[0])
                        t.wait_ge(sV.h, pT_ev[tb])
                        for j in range(4):
                            jt = bq4 * 4 + j
                            ins = t.matmul(pTp[:, tb, j, :],
                                           pn[ipar][:, jt * 128:(jt + 1) * 128],
                                           ident_sb[:], start=True, stop=True)
                        sPE.inc(ins)
                    return f
                return mk

            order = []
            for it in range(4):
                order.append(("sc", it))
                order.append(("mx", it))
                order.append(("exp", it))
                if it >= 1:
                    order.append(("T", it - 1))
                    order.append(("ev", it - 1))
                order.append(("nrm", it))
            order += [("T", 3), ("ev", 3)]

            for kind, it in order:
                if kind == "sc":
                    if ns == 1:
                        bb = rot[0] % 2
                    elif ns == 2:
                        bb = (rot[0] % 2) * 2
                    else:
                        bb = 0
                    rot[0] += 1
                    base_it[it] = bb
                    block.tensor(e_sc(it, bb))
                    sc_cnt = sPE.v
                    if filler is not None:
                        block.tensor(lambda t: filler.pe_chunk(t, 16))
                        filler.drain()
                    block.vector(e_mx(it, sc_cnt, bb))
                    ngm_cnt[it] = sV.v
                elif kind == "mx":
                    pass
                elif kind == "exp":
                    bb = base_it[it]
                    for js in range(ns):
                        bank_exp[bb + js] = sA.plan(1)
                    fimpl = e_exp(it, bb)
                    exp_defer.append(lambda s, fimpl=fimpl, ng=ngm_cnt[it],
                                     pf=pn_free[it % 2]: fimpl(s, ng=ng, pf=pf))
                    exp_cnt[it] = sA.v
                elif kind == "nrm":
                    block.vector(e_nrm(it))
                    nrm_cnt[it] = sV.v
                elif kind == "T":
                    mk = e_T(it)
                    for bq4 in range(ns):
                        tb = (it * 4 + bq4) % 2
                        block.tensor(mk(bq4, tb))
                        pec = sPE.v

                        def f_pev(it=it, bq4=bq4, tb=tb, pec=pec):
                            def f(v):
                                v.wait_ge(sPE.h, pec)
                                pT_ev[tb] = sV.inc(v.tensor_copy(
                                    out=pTs[:, bq4 * 4:(bq4 + 1) * 4, it * 128:(it + 1) * 128],
                                    in_=pTp[:, tb, :, :]))
                            return f
                        block.vector(f_pev())
                    pn_free[it % 2] = sPE.v
                elif kind == "ev":
                    pass

            pT_all = max(pT_ev)

            def f_ctx(islab=islab, h=h, g=g, ns=ns, cpar=cpar, pT_all=pT_all):
                def f(t):
                    t.wait_ge(sV.h, pT_all)
                    if h == 0 and islab == 0:
                        t.wait_ge(sV.h, v_ready_ev)
                    t.wait_ge(sV.h, ctx_ev[0])
                    for jt in range(4 * ns):
                        ins = t.matmul(ctxp[:], v_sb[:, jt, g, :], pTs[:, jt, :],
                                       start=(jt == 0), stop=(jt == 4 * ns - 1))
                    sPE.inc(ins)
                    ctx_read[0] = sPE.v
                return f
            block.tensor(f_ctx())
            ctx_cnt = sPE.v
            q_free[par] = sPE.v

            def f_cev(islab=islab, h=h, cpar=cpar, ctx_cnt=ctx_cnt):
                def f(v):
                    v.wait_ge(sPE.h, ctx_cnt)
                    ctx_ev[0] = sV.inc(v.tensor_copy(out=ctxT[:, cpar, h, :], in_=ctxp[:]))
                    ctxT_last_ev[cpar] = ctx_ev[0]
                return f
            block.vector(f_cev())

        for islab in range(4):
            exp_defer = []
            filler = WoFiller(islab - 1) if islab >= 1 else None
            for h in range(H):
                emit_head(islab, h, filler)
            if filler is not None:
                assert filler.su == 64 and not filler.post, (filler.su, len(filler.post))

            def f_exp_flush(fns=tuple(exp_defer)):
                def fn(s):
                    for g_ in fns:
                        g_(s)
                return fn
            block.scalar(f_exp_flush())
        wo_ev2[0] = wo_ev[0]
        wo_ev2[1] = ctx_ev[0]
        for su in range(64):
            emit_wo_subunit(3, su)
        barrier()
        p4.close()
        p4_sc_end = sSc.v

    if nphases >= 6:
        p5 = contextlib.ExitStack()
        h_sl = p5.enter_context(nc.sbuf_tensor("p5h", [128, NFF, 512], BF16))
        dwb = [p5.enter_context(nc.sbuf_tensor(f"p5w{i}", [128, 11, 512], BF16)) for i in range(3)]
        ds_sb = p5.enter_context(nc.sbuf_tensor("p5ds", [128, DM], F32))
        t3 = [p5.enter_context(nc.sbuf_tensor(f"p5t3{i}", [128, 512], F32)) for i in range(2)]
        gsb = [p5.enter_context(nc.sbuf_tensor(f"p5g{i}", [128, 512], BF16)) for i in range(2)]
        xb5 = [p5.enter_context(nc.sbuf_tensor(f"p5x{i}", [128, 512], F32)) for i in range(2)]
        sb5 = [p5.enter_context(nc.sbuf_tensor(f"p5s{i}", [128, 512], BF16)) for i in range(2)]
        t4 = [p5.enter_context(nc.sbuf_tensor(f"p5t4{i}", [128, 512], F32)) for i in range(2)]
        ob5 = [p5.enter_context(nc.sbuf_tensor(f"p5o{i}", [128, 512], F32)) for i in range(2)]
        dps = p5.enter_context(nc.psum_tensor("p5p", [128, 2, 4, 512], F32))

        def f_ld5():
            def f(sy):
                sQ.inc(sy.dma_start(out=ds_sb[:], in_=ds_d))
            return f
        block.sync(f_ld5())
        ds_ready = sQ.v

        hload = [0]
        h_used = [0]
        dwload = [0, 0, 0]
        dw_used = [0, 0, 0]
        bank_ev = [[0, 0, 0, 0], [0, 0, 0, 0]]
        t3_used = [0, 0]
        g_used = [0, 0]
        o_done = [0, 0]
        o_wr = [0, 0]
        t4_used = [0, 0]

        NFG = 8
        fgrp = [(a * 11, 11 if a < 7 else NFF - 77) for a in range(NFG)]

        def emit_dwl(ts, dsb, fg):
            buf = (dsb * NFG + fg) % 3

            def f(sy):
                sy.wait_ge(sPE.h, dw_used[buf])
                f0, nf = fgrp[fg]
                r0 = (dsb * NFF + f0) * 128
                sQ.inc(sy.dma_start(out=dwb[buf][:, 0:nf, :],
                                    in_=dwT_sp[r0:r0 + nf * 128, :].rearrange(
                                        "(f p) d -> p f d", p=128)))
                dwload[buf] = sQ.v
            block.sync(f)
            return sQ.v

        for ts in range(4):
            def f_hl(ts=ts):
                def f(s):
                    if ts == 0:
                        s.wait_ge(sSc.h, p4_sc_end)
                    s.wait_ge(sPE.h, h_used[0])
                    for fg in range(NFG):
                        f0, nf = fgrp[fg]
                        src = h_sp[f0 * 128:(f0 + nf) * 128, ts * 512:(ts + 1) * 512]
                        sSc.inc(s.dma_start(out=h_sl[:, f0:f0 + nf, :],
                                            in_=src.rearrange("(f p) t -> p f t", p=128)), 16)
                    hload[0] = sSc.v
                return f
            h_used[0] = sPE.v
            block.scalar(f_hl())

            dwl_cnt = {}
            dwl_cnt[(0, 0)] = emit_dwl(ts, 0, 0)
            dwl_cnt[(0, 1)] = emit_dwl(ts, 0, 1)

            for dsb in range(8):
                for fg in range(NFG):
                    nxt = dsb * NFG + fg + 2
                    if nxt < 8 * NFG:
                        dwl_cnt[(nxt // NFG, nxt % NFG)] = emit_dwl(ts, nxt // NFG, nxt % NFG)

                    def f_mm(ts=ts, dsb=dsb, fg=fg, dc=dwl_cnt[(dsb, fg)]):
                        def f(t):
                            buf = (dsb * NFG + fg) % 3
                            dpar = dsb % 2
                            t.wait_ge(sQ.h, dc)
                            if fg == 0:
                                t.wait_ge(sSc.h, hload[0])
                                for it in range(4):
                                    t.wait_ge(sV.h, bank_ev[dpar][it])
                            f0, nf = fgrp[fg]
                            for fi in range(nf):
                                f_ = f0 + fi
                                for it in range(4):
                                    ins = t.matmul(dps[:, dpar, it, :],
                                                   h_sl[:, f_, it * 128:(it + 1) * 128],
                                                   dwb[buf][:, fi, :],
                                                   start=(f_ == 0), stop=(f_ == NFF - 1))
                            sPE.inc(ins)
                            dw_used[buf] = sPE.v
                        return f
                    block.tensor(f_mm())
                mm_cnt = sPE.v

                si_defer = []
                for it in range(4):
                    par = it % 2

                    def f_t3(ts=ts, dsb=dsb, it=it, par=par, mm_cnt=mm_cnt):
                        def f(v):
                            dpar = dsb % 2
                            v.wait_ge(sPE.h, mm_cnt)
                            if ts == 0 and dsb == 0 and it == 0:
                                v.wait_ge(sQ.h, ds_ready)
                            v.wait_ge(sA.h, t3_used[par])
                            bank_ev[dpar][it] = sV.inc(v.tensor_tensor(
                                out=t3[par][:], in0=dps[:, dpar, it, :],
                                in1=ds_sb[:, dsb * 512:(dsb + 1) * 512], op=ALU.mult))
                        return f
                    block.vector(f_t3())
                    t3c = sV.v

                    sic = sA.plan(1)
                    t3_used[par] = sic

                    def d_si(par=par, t3c=t3c, gu=g_used[par]):
                        def f(s):
                            s.wait_ge(sV.h, t3c)
                            s.wait_ge(sV.h, gu)
                            ins = s.activation(out=gsb[par][:], in_=t3[par][:], func=AF.Silu)
                            sA.inc_planned(ins)
                        return f
                    si_defer.append(d_si())

                    def f_xl(ts=ts, dsb=dsb, it=it, par=par):
                        def f(g):
                            if ts == 0 and dsb == 0 and it == 0:
                                g.wait_ge(sSc.h, p4_sc_end)
                            g.wait_ge(sV.h, o_done[par])
                            r0 = ts * 512 + it * 128
                            sG.inc(g.dma_start(out=xb5[par][:],
                                               in_=x_d[r0:r0 + 128, dsb * 512:(dsb + 1) * 512]), 16)
                            sG.inc(g.dma_start(out=sb5[par][:],
                                               in_=s_sp[r0:r0 + 128, dsb * 512:(dsb + 1) * 512]), 16)
                        return f
                    block.gpsimd(f_xl())
                    xlc = sG.v

                    def f_fin(it=it, par=par, sic=sic, xlc=xlc):
                        def f(v):
                            v.wait_ge(sA.h, sic)
                            v.wait_ge(sG.h, xlc)
                            v.wait_ge(sV.h, t4_used[par])
                            ins = v.tensor_tensor(out=t4[par][:], in0=sb5[par][:], in1=gsb[par][:],
                                                  op=ALU.mult)
                            g_used[par] = sV.inc(ins)
                            v.wait_ge(sV.h, sV.v)
                            v.wait_ge(sG.h, o_wr[par])
                            ins = v.tensor_tensor(out=ob5[par][:], in0=t4[par][:], in1=xb5[par][:],
                                                  op=ALU.add)
                            o_done[par] = sV.inc(ins)
                            t4_used[par] = sV.v
                        return f
                    block.vector(f_fin())
                    fic = sV.v

                    def f_ow(ts=ts, dsb=dsb, it=it, par=par, fic=fic):
                        def f(g):
                            g.wait_ge(sV.h, fic)
                            r0 = ts * 512 + it * 128
                            sG.inc(g.dma_start(out=out_d[r0:r0 + 128, dsb * 512:(dsb + 1) * 512],
                                               in_=ob5[par][:]), 16)
                            o_wr[par] = sG.v
                        return f
                    block.gpsimd(f_ow())

                def f_si_flush(fns=tuple(si_defer)):
                    def fn(s):
                        for g_ in fns:
                            g_(s)
                    return fn
                block.scalar(f_si_flush())
            h_used[0] = sPE.v

        def f_fin_wait():
            def f(g):
                g.wait_ge(sG.h, sG.v)
            return f
        block.gpsimd(f_fin_wait())

        def f_fin_wait2():
            def f(sy):
                sy.wait_ge(sG.h, sG.v)
            return f
        block.sync(f_fin_wait2())
        p5.close()

    block_cm.__exit__(None, None, None)
    xn_es.close()
    es.close()
    return nc


def make_consts():
    inv = 1.0 / (10000.0 ** (np.arange(64, dtype=np.float64) / 64.0))
    pos = np.arange(T, dtype=np.float64)
    ang = pos[None, :] * inv[:, None]
    cos = np.cos(ang)
    sin = np.sin(ang)
    qs = 1.0 / np.sqrt(128.0)
    qcos = np.concatenate([cos, cos], 0) * qs
    qsin = np.concatenate([-sin, sin], 0) * qs
    kcos = np.concatenate([cos, cos], 0)
    ksin = np.concatenate([-sin, sin], 0)
    masks = np.zeros((128, 4, 512), np.float32)
    for r in range(4):
        i = np.arange(128)[:, None]
        jj = np.arange(512)[None, :]
        masks[:, r, :] = np.where(jj <= 128 * r + i, 0.0, NEG)
    ident = np.eye(128, dtype=ml_dtypes.bfloat16)
    return (qcos.astype(np.float32), qsin.astype(np.float32),
            kcos.astype(np.float32), ksin.astype(np.float32),
            masks.astype(ml_dtypes.bfloat16), ident)


def make_in_map(inputs, b, consts):
    qcos, qsin, kcos, ksin, masks, ident = consts
    A = np.ascontiguousarray
    x = np.asarray(inputs["L0_attn_hidden_states"])
    d = {
        "x": A(x[b].astype(np.float32, copy=False)),
        "wq": A(np.asarray(inputs["L0_attn_Wq"], np.float32)),
        "wk": A(np.asarray(inputs["L0_attn_Wk"], np.float32)),
        "wv": A(np.asarray(inputs["L0_attn_Wv"], np.float32)),
        "wo": A(np.asarray(inputs["L0_attn_Wo"], np.float32)),
        "bq2": A(np.asarray(inputs["L0_attn_bq"], np.float32).reshape(H, 128).T),
        "bk2": A(np.asarray(inputs["L0_attn_bk"], np.float32).reshape(G, 128).T),
        "bv2": A(np.asarray(inputs["L0_attn_bv"], np.float32).reshape(G, 128).T),
        "bo_b": A(np.broadcast_to(np.asarray(inputs["L0_attn_bo"], np.float32), (128, DM))),
        "gw": A(np.asarray(inputs["L0_ffn_gate_w"], np.int32)),
        "uw": A(np.asarray(inputs["L0_ffn_up_w"], np.int32)),
        "dw": A(np.asarray(inputs["L0_ffn_down_w"], np.int32)),
        "gs2": A(np.asarray(inputs["L0_ffn_gate_s"], np.float32).reshape(NFF, 128).T),
        "us2": A(np.asarray(inputs["L0_ffn_up_s"], np.float32).reshape(NFF, 128).T),
        "ds_b": A(np.broadcast_to(np.asarray(inputs["L0_ffn_down_s"], np.float32).reshape(DM), (128, DM))),
        "ident": ident,
        "qcos": A(qcos), "qsin": A(qsin), "kcos": A(kcos), "ksin": A(ksin),
        "masks": A(masks),
    }
    return d


def kernel(**inputs):
    consts = make_consts()
    nc = build_program()
    in_maps = [make_in_map(inputs, b, consts) for b in range(8)]
    res = run_bass_kernel_spmd(nc, in_maps, core_ids=list(range(8)))
    out = np.stack([np.asarray(res.results[c]["out"], np.float32) for c in range(8)], axis=0)
    return out
